# version 04
# baseline (speedup 1.0000x reference)
import contextlib
import math
import numpy as np

import concourse.bass as bass
import concourse.mybir as mybir
from concourse.bass_utils import run_bass_kernel_spmd

F32 = mybir.dt.float32
BF16 = mybir.dt.bfloat16
U8 = mybir.dt.uint8
U32 = mybir.dt.uint32
I8 = mybir.dt.int8
AF = mybir.ActivationFunctionType
ALU = mybir.AluOpType
AX = mybir.AxisListType.X

P = 128
TOK = 2048
NTT = TOK // P
D = 1024
NFC = D // P
FF = 4096
NEXP = 8
NHEAD = 16
HD = 64
NCH = TOK // P
C_RND = 1.5 * 2**23
LN_EPS = 1e-5
ATTN_EPS = 1e-5
MOE_LN_EPS = 1e-12
QNT_EPS = 1e-6
S4_N = 32
PI = math.pi

ENGS = ("sync", "scalar", "vector", "tensor", "gpsimd")
ESEM = {"sync": "qs", "scalar": "ss", "vector": "vs", "tensor": "ts", "gpsimd": "gs"}


class Sem:
    def __init__(self, es, nc, name):
        self.h = es.enter_context(nc.semaphore(name))
        self.n = 0

    def inc(self, k=1):
        self.n += k
        return self.n


def build_nc(upto=99, no_shared=False):
    nc = bass.Bass("TRN2", target_bir_lowering=False)
    SH = {} if no_shared else {"addr_space": "Shared"}

    def inp(name, shape, dt=F32):
        return nc.dram_tensor(name, list(shape), dt, kind="ExternalInput").ap()

    x_in = inp("x", [TOK, D])
    Wq_in = inp("Wq", [D, D]); Wk_in = inp("Wk", [D, D])
    Wv_in = inp("Wv", [D, D]); Wo_in = inp("Wo", [D, D])
    bo_in = inp("bo", [D])
    W8_in = inp("W8", [D, D], I8)
    wscale_in = inp("wscale", [D]); qbias_in = inp("qbias", [D])
    my_logdt_in = inp("my_logdt", [P, 1])
    my_Cre_in = inp("my_Cre", [P, S4_N]); my_Cim_in = inp("my_Cim", [P, S4_N])
    my_logAre_in = inp("my_logAre", [P, S4_N]); my_Aim_in = inp("my_Aim", [P, S4_N])
    my_D_in = inp("my_D", [P, 1])
    Ws4_in = inp("Ws4", [2 * D, D]); bs4_in = inp("bs4", [2 * D])
    rw_in = inp("rw", [NEXP, D]); rb_in = inp("rb", [NEXP])
    myw1_in = inp("myw1", [FF, D]); b1_in = inp("b1", [NEXP, FF])
    myw2_in = inp("myw2", [D, FF]); b2_in = inp("b2", [NEXP, D])
    lnw_in = inp("lnw", [D]); lnb_in = inp("lnb", [D])
    ident_in = inp("ident16", [P, P], BF16)
    pfx_in = inp("pfx", [P, 4])
    masku4_in = inp("masku4", [P, 512], BF16)
    onesk_in = inp("onesk", [P, HD], BF16)
    ones1_in = inp("ones1", [P, 1], BF16)

    out_t = nc.dram_tensor("out", [TOK, D], F32, kind="ExternalOutput").ap()

    W8T_d = nc.dram_tensor("W8T_d", [D, D], BF16).ap()
    WqT_d = nc.dram_tensor("WqT_d", [D, D], BF16).ap()
    WkT_d = nc.dram_tensor("WkT_d", [D, D], BF16).ap()
    WvT_d = nc.dram_tensor("WvT_d", [D, D], BF16).ap()
    WoT_d = nc.dram_tensor("WoT_d", [D, D], BF16).ap()
    Ws4T_d = nc.dram_tensor("Ws4T_d", [D, 2 * D], BF16).ap()
    w1T_d = nc.dram_tensor("w1T_d", [NEXP, D, FF], BF16, **SH)
    w2T_d = nc.dram_tensor("w2T_d", [NEXP, FF, D], BF16, **SH)
    ccw1_d = nc.dram_tensor("ccw1_d", [D, FF], BF16)
    ccw2_d = nc.dram_tensor("ccw2_d", [FF, D], BF16)
    attn_st_d = nc.dram_tensor("attn_st_d", [P, 520], F32)
    attn_ag_d = nc.dram_tensor("attn_ag_d", [4, P, 520], F32)
    dramT_d = nc.dram_tensor("dramT_d", [P, P, 256], BF16)
    dramE_d = nc.dram_tensor("dramE_d", [P, 2 * S4_N, P], BF16)
    dramP_d = nc.dram_tensor("dramP_d", [P, P, 2 * S4_N], BF16)
    dramL_d = nc.dram_tensor("dramL_d", [P, 2 * S4_N], F32)
    agT_d = nc.dram_tensor("agT_d", [NEXP, P, P, 256], BF16, **SH)
    agE_d = nc.dram_tensor("agE_d", [NEXP, P, 2 * S4_N, P], BF16, **SH)
    agP_d = nc.dram_tensor("agP_d", [NEXP, P, P, 2 * S4_N], BF16, **SH)
    agL_d = nc.dram_tensor("agL_d", [NEXP, P, 2 * S4_N], F32, **SH)
    s4st_d = nc.dram_tensor("s4st_d", [P, 512], F32)
    s4ag_d = nc.dram_tensor("s4ag_d", [4, P, 512], F32)

    es = contextlib.ExitStack()
    with es:
        def sb(name, shape, dt):
            return es.enter_context(nc.sbuf_tensor(name, list(shape), dt))

        def psum(name, shape, dt=F32):
            return es.enter_context(nc.psum_tensor(name, list(shape), dt))

        x_res = sb("x_res", [P, NTT, D], F32)
        A1 = sb("A1", [P, 32768], U8).ap()
        A2 = sb("A2", [P, 32768], U8).ap()
        A3 = sb("A3", [P, 32768], U8).ap()
        A4 = sb("A4", [P, 16384], U8).ap()
        A5 = sb("A5", [P, 16384], U8).ap()
        stat = sb("stat", [P, 64], F32)
        cst = sb("cst", [P, 8], F32)
        ident = sb("ident", [P, P], BF16)
        rowred = sb("rowred", [P, NTT], F32)
        small = sb("small", [P, 1024], F32)
        pfx_sb = sb("pfx_sb", [P, 4], F32)
        reps = sb("reps", [P, 2, D], F32)
        bo16 = sb("bo16", [1, D], BF16)

        def view(arena, byte0, dt, shape):
            nbytes = int(np.prod(shape)) * mybir.dt.size(dt)
            v = arena[:, byte0:byte0 + nbytes].bitcast(dt)
            if len(shape) > 1:
                dims = " ".join(f"d{i}" for i in range(len(shape)))
                kw = {f"d{i}": s for i, s in enumerate(shape)}
                v = v.rearrange(f"p ({dims}) -> p {dims}", **kw)
            return v

        pb = [psum(f"pb{i}", [P, 512], F32) for i in range(5)]
        psS = psum("psS", [P, 512], F32)
        psZ = psum("psZ", [P, 512], F32)
        pt16b = psum("pt16b", [P, 1024], BF16)

        sem = {}
        for name in ("dl", "gl", "sl", "so", "ts", "vs", "ss", "gs", "qs", "cc",
                     "t0", "t1", "g0", "g1", "io"):
            sem[name] = Sem(es, nc, name)

        ops = {k: [] for k in ENGS}
        ops_since_mark = {k: 0 for k in ENGS}
        prev_mark = {k: None for k in ENGS}

        def on(eng, is_wait=False):
            def deco(fn):
                ops[eng].append(fn)
                if not is_wait:
                    ops_since_mark[eng] += 1
                return fn
            return deco

        def wait(eng, s, v):
            if isinstance(s, str):
                s = sem[s]

            @on(eng, is_wait=True)
            def _(e, s=s, v=v):
                e.wait_ge(s.h, v)

        last_inst = {g: None for g in ENGS}
        last_marked = {g: None for g in ENGS}

        def mark(eng):
            if eng == "tensor":
                return sem["ts"].n
            if ops_since_mark[eng] == 0 and prev_mark[eng] is not None:
                return prev_mark[eng]
            s = sem[ESEM[eng]]

            @on(eng, is_wait=True)
            def _(e, s=s, eng=eng):
                li = last_inst[eng]
                if li is not None and eng != "gpsimd" and li is not last_marked[eng]:
                    li.then_inc(s.h, 1)
                    last_marked[eng] = li
                else:
                    e.nop(nofuse=True).then_inc(s.h, 1)
            ops_since_mark[eng] = 0
            prev_mark[eng] = s.inc()
            return prev_mark[eng]

        def sync_load(dst, srcfn):
            v_pre = sem["dl"].n

            @on("sync")
            def _(e, dst=dst, srcfn=srcfn, v_pre=v_pre):
                e.wait_ge(sem["dl"].h, v_pre)
                return e.dma_start(out=dst, in_=srcfn() if callable(srcfn) else srcfn).then_inc(sem["dl"].h, 16)
            return sem["dl"].inc(16)

        def gl_dma(dstfn, srcfn, s="gl"):
            v_pre = sem[s].n

            @on("gpsimd")
            def _(e, v_pre=v_pre, s=s):
                e.wait_ge(sem[s].h, v_pre)
                return e.dma_start(out=dstfn() if callable(dstfn) else dstfn,
                                   in_=srcfn() if callable(srcfn) else srcfn).then_inc(sem[s].h, 16)
            return sem[s].inc(16)

        def barrier(engs=("scalar", "vector", "tensor", "gpsimd")):
            marks = {g: mark(g) for g in engs}
            gl_now = sem["gl"].n
            for g in engs:
                for h in engs:
                    if h != g:
                        wait(g, ESEM[h], marks[h])
                if "gpsimd" in engs and g != "gpsimd" and gl_now:
                    wait(g, "gl", gl_now)

        sync_load(x_res[:], x_in.rearrange("(tt p) d -> p tt d", p=P))
        sync_load(ident[:], ident_in)
        sync_load(pfx_sb[:], pfx_in)
        sync_load(reps[:, 0, :], wscale_in.partition_broadcast(P))
        sync_load(reps[:, 1, :], qbias_in.partition_broadcast(P))
        v_base_loads = sem["dl"].n

        @on("vector")
        def _(e):
            e.memset(cst[:, 0:1], 0.0)
            e.memset(cst[:, 1:2], 1.0)
            return e.memset(cst[:, 2:8], 0.0)

        w8i = view(A3, 0, I8, (8, D))
        w8c = view(A1, 0, BF16, (8, D))
        w8t = view(A2, 0, BF16, (8, D))
        W8T_sb = view(A4, 0, BF16, (8, D))

        sync_load(w8i, W8_in.rearrange("(oc p) f -> p oc f", p=P))
        wait("vector", "dl", sem["dl"].n)

        @on("vector")
        def _(e):
            return e.tensor_copy(out=w8c, in_=w8i)
        v_w8cast = mark("vector")

        wait("tensor", "vs", v_w8cast)
        for oc in range(8):
            @on("tensor")
            def _(e, oc=oc):
                mm = None
                for fc in range(8):
                    mm = e.transpose(pt16b[:, fc * P:(fc + 1) * P],
                                     w8c[:, oc, fc * P:(fc + 1) * P], ident[:])
                return mm.then_inc(sem["ts"].h, 1)
            v_t = sem["ts"].inc()
            wait("vector", "ts", v_t)

            @on("vector")
            def _(e, oc=oc):
                return e.tensor_copy(out=w8t[:, :, oc * P:(oc + 1) * P],
                                     in_=pt16b[:].rearrange("p (fc i) -> p fc i", fc=8))
            v_e = mark("vector")
            wait("tensor", "vs", v_e)
        v_w8t_done = sem["vs"].n

        wait("gpsimd", "vs", v_w8t_done)

        @on("gpsimd")
        def _(e):
            return e.dma_start(out=W8T_d.rearrange("(fc p) o -> p fc o", p=P), in_=w8t).then_inc(sem["gl"].h, 16)
        v_w8t_store = sem["gl"].inc(16)



        def emit_wprep(src_ap, dst_dram, n_oc):
            O = n_oc * P
            wi = view(A3, 0, F32, (n_oc, D))
            wc = view(A1, 0, BF16, (n_oc, D))
            wt = view(A2, 0, BF16, (8, O))

            wait("sync", "vs", sem["vs"].n)
            wait("sync", "ts", sem["ts"].n)
            wait("sync", "gl", sem["gl"].n)

            sync_load(wi, src_ap.rearrange("(oc p) f -> p oc f", p=P))
            wait("vector", "dl", sem["dl"].n)

            @on("vector")
            def _(e):
                return e.tensor_copy(out=wc, in_=wi)
            v_c = mark("vector")
            wait("tensor", "vs", v_c)
            for oc in range(n_oc):
                @on("tensor")
                def _(e, oc=oc):
                    mm = None
                    for fc in range(8):
                        mm = e.transpose(pt16b[:, fc * P:(fc + 1) * P],
                                         wc[:, oc, fc * P:(fc + 1) * P], ident[:])
                    return mm.then_inc(sem["ts"].h, 1)
                v_t = sem["ts"].inc()
                wait("vector", "ts", v_t)

                @on("vector")
                def _(e, oc=oc):
                    return e.tensor_copy(out=wt[:, :, oc * P:(oc + 1) * P],
                                         in_=pt16b[:].rearrange("p (fc i) -> p fc i", fc=8))
                v_e = mark("vector")
                wait("tensor", "vs", v_e)
            v_done = sem["vs"].n
            wait("sync", "vs", v_done)

            @on("sync")
            def _(e):
                return e.dma_start(out=dst_dram.rearrange("(fc p) o -> p fc o", p=P),
                                   in_=wt).then_inc(sem["gl"].h, 16)
            return sem["gl"].inc(16)

        def emit_ln(dst16=None, also_uq=False, eps=LN_EPS):
            v_vec0 = sem["vs"].n
            wait("scalar", "vs", v_vec0)
            for tt in range(NTT):
                @on("vector")
                def _(e, tt=tt):
                    return e.reduce_sum(rowred[:, tt:tt + 1], x_res[:, tt, :], axis=AX)

                @on("scalar")
                def _(e, tt=tt):
                    return e.activation(out=small[:], in_=x_res[:, tt, :], func=AF.Square,
                                        accum_out=stat[:, 32 + tt:33 + tt])
            v_sq = mark("scalar")
            wait("vector", "ss", v_sq)

            @on("vector")
            def _(e):
                e.tensor_scalar(out=stat[:, 0:NTT], in0=rowred[:, 0:NTT],
                                scalar1=1.0 / D, scalar2=None, op0=ALU.mult)
                e.tensor_scalar(out=stat[:, 48:48 + NTT], in0=stat[:, 32:32 + NTT],
                                scalar1=1.0 / D, scalar2=None, op0=ALU.mult)
                e.tensor_tensor(out=small[:, 0:NTT], in0=stat[:, 0:NTT],
                                in1=stat[:, 0:NTT], op=ALU.mult)
                e.tensor_tensor(out=stat[:, 32:32 + NTT], in0=stat[:, 48:48 + NTT],
                                in1=small[:, 0:NTT], op=ALU.subtract)
                return e.tensor_scalar(out=stat[:, 32:32 + NTT], in0=stat[:, 32:32 + NTT],
                                       scalar1=eps, scalar2=None, op0=ALU.add)
            v_var = mark("vector")

            wait("scalar", "vs", v_var)

            @on("scalar")
            def _(e):
                return e.activation(out=small[:, 16:16 + NTT], in_=stat[:, 32:32 + NTT],
                                    func=AF.Sqrt)
            v_sqrt = mark("scalar")
            wait("vector", "ss", v_sqrt)

            @on("vector")
            def _(e):
                e.reciprocal(out=stat[:, 16:16 + NTT], in_=small[:, 16:16 + NTT])
                e.tensor_tensor(out=small[:, 0:NTT], in0=stat[:, 16:16 + NTT],
                                in1=stat[:, 16:16 + NTT], op=ALU.mult)
                e.tensor_tensor(out=small[:, 0:NTT], in0=small[:, 0:NTT],
                                in1=stat[:, 32:32 + NTT], op=ALU.mult)
                e.tensor_scalar(out=small[:, 0:NTT], in0=small[:, 0:NTT],
                                scalar1=-0.5, scalar2=1.5, op0=ALU.mult, op1=ALU.add)
                return e.tensor_tensor(out=stat[:, 16:16 + NTT], in0=stat[:, 16:16 + NTT],
                                       in1=small[:, 0:NTT], op=ALU.mult)
            v_rinv = mark("vector")

            if dst16 is None:
                return v_rinv

            if not also_uq:
                for tt in range(NTT):
                    @on("vector")
                    def _(e, tt=tt):
                        return e.tensor_scalar(out=dst16[:, tt, :], in0=x_res[:, tt, :],
                                               scalar1=stat[:, tt:tt + 1],
                                               scalar2=stat[:, 16 + tt:17 + tt],
                                               op0=ALU.subtract, op1=ALU.mult)
                return mark("vector")

            for tt in range(NTT):
                @on("vector")
                def _(e, tt=tt):
                    e.tensor_scalar(out=small[:], in0=x_res[:, tt, :],
                                    scalar1=stat[:, tt:tt + 1], scalar2=None,
                                    op0=ALU.subtract)
                    return e.reduce_max(rowred[:, tt:tt + 1], small[:], axis=AX,
                                        apply_absolute_value=True)

            @on("vector")
            def _(e):
                e.tensor_tensor(out=small[:, 0:NTT], in0=rowred[:, 0:NTT],
                                in1=stat[:, 16:16 + NTT], op=ALU.mult)
                e.tensor_scalar(out=small[:, 0:NTT], in0=small[:, 0:NTT],
                                scalar1=QNT_EPS, scalar2=None, op0=ALU.max)
                e.reciprocal(out=small[:, 16:16 + NTT], in_=small[:, 0:NTT])
                e.tensor_tensor(out=small[:, 16:16 + NTT], in0=small[:, 16:16 + NTT],
                                in1=stat[:, 16:16 + NTT], op=ALU.mult)
                e.tensor_scalar(out=stat[:, 48:48 + NTT], in0=small[:, 16:16 + NTT],
                                scalar1=127.0, scalar2=None, op0=ALU.mult)
                return e.tensor_scalar(out=stat[:, 32:32 + NTT], in0=small[:, 0:NTT],
                                       scalar1=0.5 / 127.0, scalar2=None, op0=ALU.mult)
            for tt in range(NTT):
                @on("vector")
                def _(e, tt=tt):
                    e.tensor_scalar(out=small[:], in0=x_res[:, tt, :],
                                    scalar1=stat[:, tt:tt + 1],
                                    scalar2=stat[:, 48 + tt:49 + tt],
                                    op0=ALU.subtract, op1=ALU.mult)
                    return e.tensor_scalar(out=dst16[:, tt, :], in0=small[:],
                                           scalar1=C_RND, scalar2=-C_RND,
                                           op0=ALU.add, op1=ALU.add)
            return mark("vector")

        def emit_transpose(src16, dstT, tts=None, pt=None, dst_tt0=0):
            if tts is None:
                tts = range(NTT)
            if pt is None:
                pt = pt16b
            for ti, tt in enumerate(tts):
                @on("tensor")
                def _(e, tt=tt, pt=pt):
                    mm = None
                    for fc in range(8):
                        mm = e.transpose(pt[:, fc * P:(fc + 1) * P],
                                         src16[:, tt, fc * P:(fc + 1) * P], ident[:])
                    return mm.then_inc(sem["ts"].h, 1)
                v_t = sem["ts"].inc()
                wait("vector", "ts", v_t)

                @on("vector")
                def _(e, tt=tt, ti=ti, pt=pt):
                    tloc = dst_tt0 + ti
                    return e.tensor_copy(
                        out=dstT[:, :, tloc * P:(tloc + 1) * P],
                        in_=pt[:].rearrange("p (fc i) -> p fc i", fc=8))
                v_e = mark("vector")
                wait("tensor", "vs", v_e)
            return sem["vs"].n

        bank_free = {}

        def wait_bank(eng, key):
            if key in bank_free:
                s, v = bank_free[key]
                wait(eng, s, v)

        def emit_qffn():
            barrier()
            uq16 = view(A1, 0, BF16, (NTT, D))
            uqT = view(A2, 0, BF16, (8, TOK))
            wait("sync", "ts", sem["ts"].n)
            v_gl_now = sem["gl"].n

            @on("sync")
            def _(e, v=v_gl_now):
                e.wait_ge(sem["gl"].h, v)
                return e.dma_start(out=W8T_sb, in_=W8T_d.rearrange("(fc p) o -> p fc o", p=P)).then_inc(sem["sl"].h, 16)
            v_w8_ld = sem["sl"].inc(16)
            v_uq = emit_ln(uq16, also_uq=True)
            wait("tensor", "vs", v_uq)
            emit_transpose(uq16, uqT)
            wait("tensor", "sl", v_w8_ld)
            for i, (tt, oh) in enumerate([(t, o) for t in range(NTT) for o in range(2)]):
                bank = i % 4
                wait_bank("tensor", bank)

                @on("tensor")
                def _(e, tt=tt, oh=oh, bank=bank):
                    mm = None
                    for fc in range(8):
                        mm = e.matmul(pb[bank][:],
                                      uqT[:, fc, tt * P:(tt + 1) * P],
                                      W8T_sb[:, fc, oh * 512:(oh + 1) * 512],
                                      start=(fc == 0), stop=(fc == 7))
                    return mm.then_inc(sem["ts"].h, 1)
                v_mm = sem["ts"].inc()
                wait("vector", "ts", v_mm)

                @on("vector")
                def _(e, tt=tt, oh=oh, bank=bank):
                    sl = slice(oh * 512, (oh + 1) * 512)
                    e.scalar_tensor_tensor(
                        out=small[:, 0:512], in0=pb[bank][:],
                        scalar=stat[:, 32 + tt:33 + tt],
                        in1=reps[:, 0, sl],
                        op0=ALU.mult, op1=ALU.mult)
                    return e.tensor_tensor(out=x_res[:, tt, sl], in0=small[:, 0:512],
                                           in1=x_res[:, tt, sl], op=ALU.add)
                bank_free[bank] = ("vs", mark("vector"))
            for tt in range(NTT):
                @on("vector")
                def _(e, tt=tt):
                    return e.scalar_tensor_tensor(out=x_res[:, tt, :], in0=reps[:, 1, :],
                                                  scalar=0.5, in1=x_res[:, tt, :],
                                                  op0=ALU.mult, op1=ALU.add)
            return mark("vector")


        def emit_attention():
            for (w_src, w_dst) in [(Wq_in, WqT_d), (Wk_in, WkT_d),
                                   (Wv_in, WvT_d), (Wo_in, WoT_d)]:
                barrier()
                emit_wprep(w_src, w_dst, 8)
            barrier()

            v_tm = view(A3, 0, BF16, (4, 1024))
            kT_tm = view(A3, 8192, BF16, (4, 1024))
            attnT = view(A3, 16384, BF16, (8, P))
            S16 = view(A3, 18432, BF16, (512,))
            S_lo = view(A3, 19456, BF16, (512,))
            S_hi = view(A3, 20480, BF16, (512,))
            zsb = view(A3, 21504, F32, (8,))
            zbl = view(A3, 21536, BF16, (8, HD))
            zbh = view(A3, 22560, BF16, (8, HD))
            rd = view(A3, 23584, F32, (1024,))
            S_in16 = view(A3, 27680, BF16, (520,))
            At = view(A3, 28720, BF16, (512,))
            masku4 = view(A3, 29744, BF16, (512,))
            onesk = view(A3, 30768, BF16, (HD,))
            gatherS = view(A3, 0, F32, (4, 520))
            u16 = view(A1, 0, BF16, (NTT, D))
            uT16q = view(A2, 0, BF16, (8, 512))
            q_fm = view(A2, 8192, BF16, (8, 512))
            k_lo = view(A2, 16384, BF16, (8, 512))
            k_hi = view(A2, 24576, BF16, (8, 512))
            WoT_sb = view(A4, 0, BF16, (8, D))
            Wst = view(A5, 0, BF16, (8, D))

            wait("sync", "vs", sem["vs"].n)
            wait("sync", "ts", sem["ts"].n)

            sync_load(masku4, masku4_in)
            sync_load(onesk, onesk_in)
            sync_load(small[0:1, 0:D], bo_in)
            wait("vector", "dl", sem["dl"].n)

            @on("vector")
            def _(e):
                e.tensor_copy(out=bo16[:], in_=small[0:1, 0:D])
                e.memset(k_lo[64:128, :, :], 0.0)
                e.memset(k_hi[0:64, :, :], 0.0)
                e.memset(S_lo[64:128, :], 0.0)
                e.memset(S_hi[0:64, :], 0.0)
                e.memset(zbl[64:128, :, :], 0.0)
                return e.memset(zbh[0:64, :, :], 0.0)
            mark("vector")

            v_u16 = emit_ln(u16)

            v_gl_pre = sem["gl"].n
            wait("sync", "ts", sem["ts"].n)

            @on("sync")
            def _(e, v=v_gl_pre):
                e.wait_ge(sem["gl"].h, v)
                return e.dma_start(out=WoT_sb, in_=WoT_d.rearrange("(fc p) o -> p fc o", p=P)).then_inc(sem["sl"].h, 16)
            v_wo_ld = sem["sl"].inc(16)

            def load_w(dram):
                wait("sync", "ts", sem["ts"].n)
                wait("sync", "gl", sem["gl"].n)
                return sync_load(Wst, dram.rearrange("(fc p) o -> p fc o", p=P))

            def quarter_transpose(qq):
                wait("tensor", "vs", v_u16)
                return emit_transpose(u16, uT16q, tts=range(qq * 4, qq * 4 + 4))

            def phi_evac(bank, outs):
                @on("vector")
                def _(e, bank=bank):
                    return e.tensor_scalar_min(small[:, 0:512], pb[bank][:], 0.0)
                v1 = mark("vector")
                wait("scalar", "vs", v1)

                @on("scalar")
                def _(e):
                    return e.activation(out=small[:, 512:1024], in_=small[:, 0:512],
                                        func=AF.Exp)
                v2 = mark("scalar")
                wait("vector", "ss", v2)
                for dst, rows in outs:
                    @on("vector")
                    def _(e, dst=dst, rows=rows, bank=bank):
                        return e.scalar_tensor_tensor(
                            out=dst, in0=pb[bank][rows[0]:rows[1], :],
                            scalar=0.0, in1=small[rows[0]:rows[1], 512:1024],
                            op0=ALU.max, op1=ALU.add)
                return mark("vector")

            def proj_fm(v_wld, phi, dual, v_uT):
                for Ec in range(8):
                    bank = Ec % 2
                    wait("tensor", "dl", v_wld)
                    wait("tensor", "vs", v_uT)
                    wait_bank("tensor", bank)

                    @on("tensor")
                    def _(e, Ec=Ec, bank=bank):
                        mm = None
                        for fc in range(8):
                            mm = e.matmul(pb[bank][:], Wst[:, fc, Ec * P:(Ec + 1) * P],
                                          uT16q[:, fc, :], start=(fc == 0), stop=(fc == 7),
                                          skip_group_check=True)
                        return mm.then_inc(sem["ts"].h, 1)
                    v_mm = sem["ts"].inc()
                    wait("vector", "ts", v_mm)
                    if dual:
                        bank_free[bank] = ("vs", phi_evac(bank, [(k_lo[0:64, Ec, :], (0, 64)),
                                                                 (k_hi[64:128, Ec, :], (64, 128))]))
                    else:
                        bank_free[bank] = ("vs", phi_evac(bank, [(q_fm[:, Ec, :], (0, 128))]))
                return max(bank_free[0][1], bank_free[1][1])

            def proj_tm(v_wld, phi, dst, v_uT):
                for tl in range(4):
                    for oh in range(2):
                        bank = (tl * 2 + oh) % 2
                        wait("tensor", "dl", v_wld)
                        wait("tensor", "vs", v_uT)
                        wait_bank("tensor", bank)

                        @on("tensor")
                        def _(e, tl=tl, oh=oh, bank=bank):
                            mm = None
                            for fc in range(8):
                                mm = e.matmul(pb[bank][:], uT16q[:, fc, tl * P:(tl + 1) * P],
                                              Wst[:, fc, oh * 512:(oh + 1) * 512],
                                              start=(fc == 0), stop=(fc == 7),
                                              skip_group_check=True)
                            return mm.then_inc(sem["ts"].h, 1)
                        v_mm = sem["ts"].inc()
                        wait("vector", "ts", v_mm)
                        if phi:
                            bank_free[bank] = ("vs", phi_evac(bank, [(dst[:, tl, oh * 512:(oh + 1) * 512], (0, 128))]))
                        else:
                            @on("vector")
                            def _(e, tl=tl, oh=oh, bank=bank):
                                return e.tensor_copy(out=dst[:, tl, oh * 512:(oh + 1) * 512],
                                                     in_=pb[bank][:])
                            bank_free[bank] = ("vs", mark("vector"))
                return max(bank_free[0][1], bank_free[1][1])

            def supd(c, v_kv, b1):
                cq = c % 4
                wait("tensor", "vs", v_kv)
                @on("tensor")
                def _(e, c=c, cq=cq, b1=b1):
                    mm = None
                    for h in range(NHEAD):
                        par = h % 2
                        h2 = h // 2
                        st = b1 and (c == 0) and (h < 2)
                        sp = (c == 15) and (h >= 14)
                        tp = (0, 64 * par)
                        mm = e.matmul(psS[par * 64:par * 64 + 64, h2 * 64:(h2 + 1) * 64],
                                      kT_tm[:, cq, h * 64:(h + 1) * 64],
                                      v_tm[:, cq, h * 64:(h + 1) * 64],
                                      start=st, stop=sp, tile_position=tp,
                                      skip_group_check=True)
                        mm = e.matmul(psZ[par * 64:par * 64 + 64, h2:h2 + 1],
                                      kT_tm[:, cq, h * 64:(h + 1) * 64],
                                      onesk[:, 0:1],
                                      start=st, stop=sp, tile_position=tp,
                                      skip_group_check=True)
                    return mm.then_inc(sem["ts"].h, 1)
                return sem["ts"].inc()

            for qq in range(4):
                v_uT = quarter_transpose(qq)
                v_wk = load_w(WkT_d)
                v_kT = proj_tm(v_wk, True, kT_tm, v_uT)
                v_wv = load_w(WvT_d)
                v_v = proj_tm(v_wv, False, v_tm, v_uT)
                for cl in range(4):
                    supd(qq * 4 + cl, max(v_kT, v_v), True)

            v_b1 = sem["ts"].n
            wait("vector", "ts", v_b1)

            @on("vector")
            def _(e):
                e.tensor_copy(out=small[:, 0:512], in_=psS[:])
                return e.tensor_copy(out=small[:, 512:520], in_=psZ[:, 0:8])
            v_sev = mark("vector")
            wait("gpsimd", "vs", v_sev)

            @on("gpsimd")
            def _(e):
                return e.dma_start(out=attn_st_d.ap(), in_=small[:, 0:520]).then_inc(sem["gl"].h, 16)
            v_st = sem["gl"].inc(16)

            @on("gpsimd")
            def _(e, v=v_st):
                e.wait_ge(sem["gl"].h, v)
                return e.collective_compute(
                    "AllGather", ALU.bypass,
                    replica_groups=[[0, 1, 2, 3], [4, 5, 6, 7]],
                    ins=[attn_st_d.ap()], outs=[attn_ag_d.ap()],
                ).then_inc(sem["cc"].h, 1)
            v_cc = sem["cc"].inc()

            @on("gpsimd")
            def _(e, v=v_cc):
                e.wait_ge(sem["cc"].h, v)
                return e.dma_start(out=gatherS, in_=attn_ag_d.ap().rearrange("s p c -> p s c")).then_inc(sem["gl"].h, 16)
            v_ga = sem["gl"].inc(16)
            wait("vector", "gl", v_ga)

            @on("vector")
            def _(e):
                e.memset(small[:, 0:520], 0.0)
                for i in range(4):
                    e.drain()
                    e.scalar_tensor_tensor(out=small[:, 0:520], in0=gatherS[:, i, :],
                                           scalar=pfx_sb[:, i:i + 1], in1=small[:, 0:520],
                                           op0=ALU.mult, op1=ALU.add)
                e.drain()
                return e.tensor_copy(out=S_in16, in_=small[:, 0:520])
            v_sin = mark("vector")
            wait("tensor", "vs", v_sin)

            @on("tensor")
            def _(e):
                e.matmul(psS[:], ident[:], S_in16[:, 0:512], start=True, stop=False,
                         skip_group_check=True)
                return e.matmul(psZ[:, 0:8], ident[:], S_in16[:, 512:520], start=True,
                                stop=False, skip_group_check=True).then_inc(sem["ts"].h, 1)
            sem["ts"].inc()


            def chunk_body(c):
                cq = c % 4
                v_prior = sem["ts"].n
                wait("vector", "ts", v_prior)

                @on("vector")
                def _(e):
                    e.tensor_copy(out=S16, in_=psS[:])
                    e.drain()
                    e.tensor_copy(out=S_lo[0:64, :], in_=S16[0:64, :])
                    e.tensor_copy(out=S_hi[64:128, :], in_=S16[64:128, :])
                    e.tensor_copy(out=zsb, in_=psZ[:, 0:8])
                    e.drain()
                    last = None
                    for h2 in range(8):
                        e.tensor_copy(out=zbl[0:64, h2, :],
                                      in_=zsb[0:64, h2:h2 + 1].to_broadcast([64, HD]))
                        last = e.tensor_copy(out=zbh[64:128, h2, :],
                                             in_=zsb[64:128, h2:h2 + 1].to_broadcast([64, HD]))
                    return last
                v_state = mark("vector")
                wait("tensor", "vs", v_state)

                start_seen = set()

                def stflag(bank, par):
                    key = (bank, par)
                    if key in start_seen:
                        return False
                    start_seen.add(key)
                    return True

                for g in range(4):
                    heads = range(4 * g, 4 * g + 4)
                    wait_bank("tensor", 0)

                    @on("tensor")
                    def _(e, heads=heads, cq=cq):
                        mm = None
                        for ih, h in enumerate(heads):
                            kp = k_lo if h % 2 == 0 else k_hi
                            mm = e.matmul(pb[0][:, ih * P:(ih + 1) * P],
                                          kp[:, h // 2, cq * P:(cq + 1) * P],
                                          q_fm[:, h // 2, cq * P:(cq + 1) * P],
                                          start=(ih == 0), stop=(ih == 3),
                                          skip_group_check=True)
                        return mm.then_inc(sem["ts"].h, 1)
                    v_a = sem["ts"].inc()
                    wait("vector", "ts", v_a)

                    @on("vector")
                    def _(e):
                        return e.tensor_tensor(out=At, in0=pb[0][:], in1=masku4,
                                               op=ALU.mult)
                    bank_free[0] = ("vs", mark("vector"))
                    wait_bank("tensor", 0)

                    @on("tensor")
                    def _(e, heads=heads, cq=cq, g=g):
                        mm = None
                        for ih, h in enumerate(heads):
                            par = h % 2
                            h2 = h // 2
                            dbank = 1 if h < 8 else 2
                            obk = 3 if h < 8 else 4
                            col = (h2 % 4) * P
                            tp = (0, 64 * par)
                            rows = slice(par * 64, par * 64 + 64)
                            zb = zbl if par == 0 else zbh
                            Sp = S_lo if par == 0 else S_hi
                            mm = e.matmul(pb[dbank][rows, col:col + P], onesk[:, 0:HD],
                                          At[:, ih * P:(ih + 1) * P],
                                          start=stflag(dbank, par), stop=False,
                                          tile_position=tp, skip_group_check=True)
                            mm = e.matmul(pb[dbank][rows, col:col + P], zb[:, h2, :],
                                          q_fm[:, h2, cq * P:(cq + 1) * P],
                                          start=False, stop=False,
                                          tile_position=tp, skip_group_check=True)
                            mm = e.matmul(pb[obk][rows, col:col + P],
                                          v_tm[:, cq, h * HD:(h + 1) * HD],
                                          At[:, ih * P:(ih + 1) * P],
                                          start=stflag(obk, par), stop=False,
                                          tile_position=tp, skip_group_check=True)
                            mm = e.matmul(pb[obk][rows, col:col + P],
                                          Sp[:, h2 * HD:(h2 + 1) * HD],
                                          q_fm[:, h2, cq * P:(cq + 1) * P],
                                          start=False, stop=False,
                                          tile_position=tp, skip_group_check=True)
                            sp = (c == 15) and (h >= 14)
                            mm = e.matmul(psS[rows, h2 * 64:(h2 + 1) * 64],
                                          kT_tm[:, cq, h * HD:(h + 1) * HD],
                                          v_tm[:, cq, h * HD:(h + 1) * HD],
                                          start=False, stop=sp, tile_position=tp,
                                          skip_group_check=True)
                            mm = e.matmul(psZ[rows, h2:h2 + 1],
                                          kT_tm[:, cq, h * HD:(h + 1) * HD],
                                          onesk[:, 0:1],
                                          start=False, stop=sp, tile_position=tp,
                                          skip_group_check=True)
                        return mm.then_inc(sem["ts"].h, 1)
                    sem["ts"].inc()

                v_groups = sem["ts"].n
                wait("vector", "ts", v_groups)

                @on("vector")
                def _(e):
                    e.tensor_scalar(out=rd[:, 0:512], in0=pb[1][:], scalar1=ATTN_EPS,
                                    scalar2=None, op0=ALU.add)
                    e.drain()
                    e.reciprocal(out=rd[:, 0:512], in_=rd[:, 0:512])
                    e.tensor_scalar(out=rd[:, 512:1024], in0=pb[2][:], scalar1=ATTN_EPS,
                                    scalar2=None, op0=ALU.add)
                    e.drain()
                    e.reciprocal(out=rd[:, 512:1024], in_=rd[:, 512:1024])
                    e.drain()
                    e.tensor_tensor(out=attnT[:, 0:4, :], in0=pb[3][:],
                                    in1=rd[:, 0:512], op=ALU.mult)
                    e.drain()
                    return e.tensor_tensor(out=attnT[:, 4:8, :], in0=pb[4][:],
                                           in1=rd[:, 512:1024], op=ALU.mult)
                v_attnT = mark("vector")
                wait("tensor", "vs", v_attnT)
                wait("tensor", "sl", v_wo_ld)

                for oh in range(2):
                    @on("tensor")
                    def _(e, oh=oh):
                        mm = None
                        for fc in range(8):
                            mm = e.matmul(pb[0][:], attnT[:, fc, :],
                                          WoT_sb[:, fc, oh * 512:(oh + 1) * 512],
                                          start=(fc == 0), stop=False,
                                          skip_group_check=True)
                        mm = e.matmul(pb[0][:], masku4[0:1, 0:P],
                                      bo16[0:1, oh * 512:(oh + 1) * 512],
                                      start=False, stop=True, skip_group_check=True)
                        return mm.then_inc(sem["ts"].h, 1)
                    v_wo = sem["ts"].inc()
                    wait("vector", "ts", v_wo)

                    @on("vector")
                    def _(e, oh=oh, c=c):
                        sl = slice(oh * 512, (oh + 1) * 512)
                        return e.scalar_tensor_tensor(out=x_res[:, c, sl], in0=pb[0][:],
                                                      scalar=0.0, in1=x_res[:, c, sl],
                                                      op0=ALU.bypass, op1=ALU.add)
                    bank_free[0] = ("vs", mark("vector"))
                    wait_bank("tensor", 0)

            for qq in range(4):
                v_uT = quarter_transpose(qq)
                v_wq = load_w(WqT_d)
                v_q = proj_fm(v_wq, True, False, v_uT)
                v_wk = load_w(WkT_d)
                v_kp = proj_fm(v_wk, True, True, v_uT)
                v_kT = proj_tm(v_wk, True, kT_tm, v_uT)
                v_wv = load_w(WvT_d)
                v_v = proj_tm(v_wv, False, v_tm, v_uT)
                wait("tensor", "vs", max(v_q, v_kp, v_kT, v_v))
                for cl in range(4):
                    chunk_body(qq * 4 + cl)
            return None


        def emit_wprep2(src2d, dst2d, n_oc, n_fc):
            wi = view(A3, 0, F32, (n_oc, n_fc * P))
            wc = view(A1, 0, BF16, (n_oc, n_fc * P))
            wt = view(A2, 0, BF16, (n_fc, n_oc * P))
            wait("sync", "vs", sem["vs"].n)
            wait("sync", "ts", sem["ts"].n)
            wait("sync", "gl", sem["gl"].n)

            sync_load(wi, src2d.rearrange("(oc p) f -> p oc f", p=P))
            wait("vector", "dl", sem["dl"].n)

            @on("vector")
            def _(e):
                return e.tensor_copy(out=wc, in_=wi)
            v_c = mark("vector")
            wait("tensor", "vs", v_c)
            for oc in range(n_oc):
                @on("tensor")
                def _(e, oc=oc):
                    mm = None
                    for fc in range(n_fc):
                        mm = e.transpose(pt16b[:, fc * P:(fc + 1) * P],
                                         wc[:, oc, fc * P:(fc + 1) * P], ident[:])
                    return mm.then_inc(sem["ts"].h, 1)
                v_t = sem["ts"].inc()
                wait("vector", "ts", v_t)

                @on("vector")
                def _(e, oc=oc, n_fc=n_fc):
                    return e.tensor_copy(
                        out=wt[:, :, oc * P:(oc + 1) * P],
                        in_=pt16b[:, 0:n_fc * P].rearrange("p (fc i) -> p fc i", fc=n_fc))
                v_e = mark("vector")
                wait("tensor", "vs", v_e)
            wait("gpsimd", "vs", sem["vs"].n)

            @on("gpsimd")
            def _(e):
                return e.dma_start(out=dst2d.rearrange("(fc p) o -> p fc o", p=P),
                                   in_=wt).then_inc(sem["gl"].h, 16)
            return sem["gl"].inc(16)

        def emit_moe_prestage():
            for s in range(8):
                emit_wprep2(myw1_in[s * 512:(s + 1) * 512, :],
                            ccw1_d.ap()[:, s * 512:(s + 1) * 512], 4, 8)
            for s in range(8):
                emit_wprep2(myw2_in[:, s * 512:(s + 1) * 512],
                            ccw2_d.ap()[s * 512:(s + 1) * 512, :], 8, 4)
            v_gl = sem["gl"].n
            wait("gpsimd", "gl", v_gl)

            @on("gpsimd")
            def _(e):
                e.collective_compute(
                    "AllGather", ALU.bypass, replica_groups=[list(range(8))],
                    ins=[ccw1_d.ap()], outs=[w1T_d.ap()]).then_inc(sem["cc"].h, 1)
                return e.collective_compute(
                    "AllGather", ALU.bypass, replica_groups=[list(range(8))],
                    ins=[ccw2_d.ap()], outs=[w2T_d.ap()]).then_inc(sem["cc"].h, 1)
            return sem["cc"].inc(2)

        def emit_moe(v_ccw):
            barrier()
            u16q = view(A3, 0, BF16, (4, D))
            ulo_t = view(A3, 8192, BF16, (4, D))
            hlnw = view(A3, 16384, F32, (D,))
            hlnb = view(A3, 20480, F32, (D,))
            rbrep = view(A3, 24576, F32, (8,))
            logits32 = view(A3, 24608, F32, (4, 8))
            cw32 = view(A3, 24736, F32, (4, 8))
            mx8 = view(A3, 24864, F32, (8,))
            idx8 = view(A3, 24896, U32, (8,))
            p12 = view(A3, 24928, F32, (8,))
            b1col = view(A3, 24960, F32, (32,))
            b1all = view(A3, 30528, F32, (NEXP * 32,))
            b2t = view(A3, 25088, BF16, (D,))
            rw16 = view(A3, 27136, BF16, (D,))
            rwt = view(A3, 29184, BF16, (8, 8))
            rwtlo = view(A3, 29312, BF16, (8, 8))
            cwt = view(A3, 29440, BF16, (512,))
            cw16 = view(A3, 30464, BF16, (4, 8))
            rwlo16 = view(A3, 0, BF16, (D,))
            rwlo32 = view(A3, 4096, F32, (D,))
            uT16q = view(A2, 0, BF16, (8, 512))
            uTlo16q = view(A2, 8192, BF16, (8, 512))
            hT = view(A1, 0, BF16, (32, 512))
            u32q = view(A4, 0, F32, (4, D))
            b2f = view(A4, 0, F32, (2 * D,))
            w1w = [view(A5, 0, BF16, (8, 512)), view(A5, 8192, BF16, (8, 512))]
            w2w = [view(A5, 0, BF16, (4, D)), view(A5, 8192, BF16, (4, D))]

            wait("gpsimd", "cc", v_ccw)
            wait("sync", "cc", v_ccw)

            wait("sync", "vs", sem["vs"].n)
            wait("sync", "ts", sem["ts"].n)

            sync_load(hlnw, lnw_in.partition_broadcast(P))
            sync_load(hlnb, lnb_in.partition_broadcast(P))
            sync_load(rbrep, rb_in.partition_broadcast(P))
            sync_load(small[0:8, 0:D], rw_in)
            sync_load(b2f[0:8, 0:D], b2_in)
            v_cst = sem["dl"].n
            wait("vector", "dl", v_cst)

            @on("vector")
            def _(e):
                e.tensor_scalar(out=hlnw, in0=hlnw, scalar1=0.5, scalar2=None, op0=ALU.mult)
                e.tensor_scalar(out=hlnb, in0=hlnb, scalar1=0.5, scalar2=None, op0=ALU.mult)
                e.tensor_copy(out=b2t[0:8, :], in_=b2f[0:8, 0:D])
                e.tensor_copy(out=rw16[0:8, :], in_=small[0:8, 0:D])
                e.tensor_copy(out=rwlo32[0:8, :], in_=rw16[0:8, :])
                e.tensor_tensor(out=rwlo32[0:8, :], in0=small[0:8, 0:D],
                                in1=rwlo32[0:8, :], op=ALU.subtract)
                return e.tensor_copy(out=rwlo16[0:8, :], in_=rwlo32[0:8, :])
            v_rw = mark("vector")
            wait("tensor", "vs", v_rw)

            for fc in range(8):
                @on("tensor")
                def _(e, fc=fc):
                    e.transpose(pt16b[:, 0:8], rw16[0:8, fc * P:(fc + 1) * P], ident[0:8, 0:8])
                    return e.transpose(pt16b[:, 8:16], rwlo16[0:8, fc * P:(fc + 1) * P],
                                       ident[0:8, 0:8]).then_inc(sem["ts"].h, 1)
                v_t = sem["ts"].inc()
                wait("vector", "ts", v_t)

                @on("vector")
                def _(e, fc=fc):
                    e.tensor_copy(out=rwt[:, fc, :], in_=pt16b[:, 0:8])
                    return e.tensor_copy(out=rwtlo[:, fc, :], in_=pt16b[:, 8:16])
                v_e = mark("vector")
                wait("tensor", "vs", v_e)

            for ex in range(NEXP):
                wait("sync", "vs", sem["vs"].n)

                v_b = sync_load(small[0:32, 0:P], b1_in[ex, :].rearrange("(a b) -> a b", b=P))
                wait("vector", "dl", v_b)

                @on("vector")
                def _(e):
                    return e.tensor_copy(out=rwlo16[0:32, 0:P], in_=small[0:32, 0:P])
                v_bc = mark("vector")
                wait("tensor", "vs", v_bc)

                @on("tensor")
                def _(e):
                    return e.transpose(pt16b[:, 0:32], rwlo16[0:32, 0:P],
                                       ident[0:32, 0:32]).then_inc(sem["ts"].h, 1)
                v_bt = sem["ts"].inc()
                wait("vector", "ts", v_bt)

                @on("vector")
                def _(e, ex=ex):
                    return e.tensor_copy(out=b1all[:, ex * 32:(ex + 1) * 32],
                                         in_=pt16b[:, 0:32])
                v_be = mark("vector")
                wait("tensor", "vs", v_be)

            emit_ln(None)

            buf_busy = {}

            for qq in range(4):
                wait("vector", "ts", sem["ts"].n)
                for tl in range(4):
                    tt = qq * 4 + tl

                    @on("vector")
                    def _(e, tl=tl, tt=tt):
                        e.tensor_scalar(out=u32q[:, tl, :], in0=x_res[:, tt, :],
                                        scalar1=stat[:, tt:tt + 1],
                                        scalar2=stat[:, 16 + tt:17 + tt],
                                        op0=ALU.subtract, op1=ALU.mult)
                        e.tensor_copy(out=u16q[:, tl, :], in_=u32q[:, tl, :])
                        e.tensor_copy(out=small[:, 0:D], in_=u16q[:, tl, :])
                        return e.tensor_tensor(out=ulo_t[:, tl, :], in0=u32q[:, tl, :],
                                               in1=small[:, 0:D], op=ALU.subtract)
                v_u = mark("vector")
                wait("tensor", "vs", v_u)
                emit_transpose(u16q, uT16q, tts=range(4))
                emit_transpose(ulo_t, uTlo16q, tts=range(4))

                for tl in range(4):
                    wait_bank("tensor", 4)
                    wait("tensor", "dl", v_cst)

                    @on("tensor")
                    def _(e, tl=tl):
                        mm = None
                        for fc in range(8):
                            mm = e.matmul(pb[4][:, 0:8], uT16q[:, fc, tl * P:(tl + 1) * P],
                                          rwt[:, fc, :], start=(fc == 0), stop=False,
                                          skip_group_check=True)
                        for fc in range(8):
                            mm = e.matmul(pb[4][:, 0:8], uT16q[:, fc, tl * P:(tl + 1) * P],
                                          rwtlo[:, fc, :], start=False, stop=False,
                                          skip_group_check=True)
                        for fc in range(8):
                            mm = e.matmul(pb[4][:, 0:8], uTlo16q[:, fc, tl * P:(tl + 1) * P],
                                          rwt[:, fc, :], start=False, stop=(fc == 7),
                                          skip_group_check=True)
                        return mm.then_inc(sem["ts"].h, 1)
                    v_lg = sem["ts"].inc()
                    wait("vector", "ts", v_lg)

                    @on("vector")
                    def _(e, tl=tl):
                        return e.tensor_tensor(out=logits32[:, tl, :], in0=pb[4][:, 0:8],
                                               in1=rbrep, op=ALU.add)
                    bank_free[4] = ("vs", mark("vector"))

                    @on("vector")
                    def _(e, tl=tl):
                        e.max(out=mx8, in_=logits32[:, tl, :])
                        e.max_index(out=idx8, in_max=mx8, in_values=logits32[:, tl, :])
                        return e.tensor_tensor(out=p12[:, 0:1], in0=mx8[:, 0:1],
                                               in1=mx8[:, 1:2], op=ALU.subtract)
                    v_mx = mark("vector")
                    wait("scalar", "vs", v_mx)

                    @on("scalar")
                    def _(e):
                        return e.activation(out=p12[:, 2:3], in_=p12[:, 0:1],
                                            func=AF.Sigmoid)
                    v_sg = mark("scalar")
                    wait("vector", "ss", v_sg)

                    @on("vector")
                    def _(e, tl=tl):
                        e.tensor_scalar(out=p12[:, 3:4], in0=p12[:, 2:3], scalar1=-1.0,
                                        scalar2=1.0, op0=ALU.mult, op1=ALU.add)
                        e.tensor_copy(out=p12[:, 4:6], in_=idx8[:, 0:2])
                        for ex in range(NEXP):
                            e.tensor_scalar(out=p12[:, 6:7], in0=p12[:, 4:5],
                                            scalar1=float(ex), scalar2=None,
                                            op0=ALU.is_equal)
                            e.tensor_scalar(out=p12[:, 7:8], in0=p12[:, 5:6],
                                            scalar1=float(ex), scalar2=None,
                                            op0=ALU.is_equal)
                            e.tensor_tensor(out=p12[:, 6:7], in0=p12[:, 6:7],
                                            in1=p12[:, 2:3], op=ALU.mult)
                            e.tensor_tensor(out=p12[:, 7:8], in0=p12[:, 7:8],
                                            in1=p12[:, 3:4], op=ALU.mult)
                            e.tensor_tensor(out=cw32[:, tl, ex:ex + 1], in0=p12[:, 6:7],
                                            in1=p12[:, 7:8], op=ALU.add)
                        return e.tensor_copy(out=cw16[:, tl, :], in_=cw32[:, tl, :])
                    v_cw = mark("vector")
                    wait("tensor", "vs", v_cw)

                    @on("tensor")
                    def _(e, tl=tl):
                        return e.transpose(pt16b[0:8, 0:P], cw16[:, tl, :],
                                           ident[:]).then_inc(sem["ts"].h, 1)
                    v_ct = sem["ts"].inc()
                    wait("vector", "ts", v_ct)

                    @on("vector")
                    def _(e, tl=tl):
                        return e.tensor_copy(out=cwt[0:8, tl * P:(tl + 1) * P],
                                             in_=pt16b[0:8, 0:P])
                    bank_free["ct"] = ("vs", mark("vector"))
                    wait_bank("tensor", "ct")

                for ex in range(NEXP):
                    wait("scalar", "ts", sem["ts"].n)

                    for wnd in range(8):
                        buf = w1w[wnd % 2]
                        key = wnd % 2
                        if key in buf_busy:
                            wait("sync", "ts", buf_busy[key])

                        v_w = sync_load(buf, w1T_d.ap()[ex].rearrange("(fc p) o -> p fc o", p=P)[:, :, wnd * 512:(wnd + 1) * 512])
                        wait("tensor", "dl", v_w)
                        for f2 in range(4):
                            ffc = wnd * 4 + f2
                            pbank = pb[0] if ffc % 2 == 0 else psS
                            bk = 0 if ffc % 2 == 0 else 5
                            wait_bank("tensor", bk)

                            @on("tensor")
                            def _(e, f2=f2, pbank=pbank, buf=buf):
                                mm = None
                                for fc in range(8):
                                    mm = e.matmul(pbank[:], buf[:, fc, f2 * P:(f2 + 1) * P],
                                                  uT16q[:, fc, :], start=(fc == 0),
                                                  stop=(fc == 7), skip_group_check=True)
                                return mm.then_inc(sem["ts"].h, 1)
                            v_mm = sem["ts"].inc()
                            buf_busy[key] = v_mm
                            wait("scalar", "ts", v_mm)

                            @on("scalar")
                            def _(e, ffc=ffc, pbank=pbank, ex=ex):
                                return e.activation(out=hT[:, ffc, :], in_=pbank[:],
                                                    func=AF.Gelu,
                                                    bias=b1all[:, ex * 32 + ffc:ex * 32 + ffc + 1])
                            bank_free[bk] = ("ss", mark("scalar"))
                    v_hT = sem["ss"].n
                    wait("tensor", "ss", v_hT)

                    for pair in range(2):
                        tls = (2 * pair, 2 * pair + 1)
                        for wnd in range(8):
                            buf = w2w[wnd % 2]
                            key = wnd % 2
                            if key in buf_busy:
                                wait("sync", "ts", buf_busy[key])

                            v_w = sync_load(buf, w2T_d.ap()[ex, wnd * 512:(wnd + 1) * 512, :].rearrange("(fc p) o -> p fc o", p=P))
                            wait("tensor", "dl", v_w)
                            if wnd == 0:
                                for ip in range(1, 5):
                                    wait_bank("tensor", ip)

                            @on("tensor")
                            def _(e, wnd=wnd, buf=buf, tls=tls):
                                mm = None
                                for f2 in range(4):
                                    ffc = wnd * 4 + f2
                                    for it, tl in enumerate(tls):
                                        for oh in range(2):
                                            mm = e.matmul(
                                                pb[1 + it * 2 + oh][:],
                                                hT[:, ffc, tl * P:(tl + 1) * P],
                                                buf[:, f2, oh * 512:(oh + 1) * 512],
                                                start=(ffc == 0), stop=(ffc == 31),
                                                skip_group_check=True)
                                return mm.then_inc(sem["ts"].h, 1)
                            v_mm = sem["ts"].inc()
                            buf_busy[key] = v_mm
                        wait("vector", "ts", sem["ts"].n)
                        for it, tl in enumerate(tls):
                            for oh in range(2):
                                @on("vector")
                                def _(e, it=it, tl=tl, oh=oh, ex=ex):
                                    sl = slice(oh * 512, (oh + 1) * 512)
                                    return e.scalar_tensor_tensor(
                                        out=u32q[:, tl, sl], in0=pb[1 + it * 2 + oh][:],
                                        scalar=cw32[:, tl, ex:ex + 1],
                                        in1=u32q[:, tl, sl],
                                        op0=ALU.mult, op1=ALU.add)
                                bank_free[1 + it * 2 + oh] = ("vs", mark("vector"))

                for tl in range(4):
                    for ip in (3, 4):
                        wait_bank("tensor", ip)

                    @on("tensor")
                    def _(e, tl=tl):
                        e.matmul(pb[4][:], cwt[0:8, tl * P:(tl + 1) * P],
                                 b2t[0:8, 0:512], start=True, stop=True,
                                 skip_group_check=True)
                        mm = e.matmul(pb[3][:], cwt[0:8, tl * P:(tl + 1) * P],
                                      b2t[0:8, 512:1024], start=True, stop=True,
                                      skip_group_check=True)
                        return mm.then_inc(sem["ts"].h, 1)
                    v2 = sem["ts"].inc()
                    wait("vector", "ts", v2)

                    @on("vector")
                    def _(e, tl=tl):
                        e.scalar_tensor_tensor(out=u32q[:, tl, 0:512], in0=pb[4][:],
                                               scalar=0.0, in1=u32q[:, tl, 0:512],
                                               op0=ALU.bypass, op1=ALU.add)
                        return e.scalar_tensor_tensor(out=u32q[:, tl, 512:1024], in0=pb[3][:],
                                                      scalar=0.0, in1=u32q[:, tl, 512:1024],
                                                      op0=ALU.bypass, op1=ALU.add)
                    bank_free[4] = ("vs", mark("vector"))
                    bank_free[3] = bank_free[4]

                wait("scalar", "vs", sem["vs"].n)
                for tl in range(4):
                    @on("vector")
                    def _(e, tl=tl):
                        return e.reduce_sum(rowred[:, tl:tl + 1], u32q[:, tl, :], axis=AX)

                    @on("scalar")
                    def _(e, tl=tl):
                        return e.activation(out=small[:], in_=u32q[:, tl, :],
                                            func=AF.Square,
                                            accum_out=stat[:, 36 + tl:37 + tl])
                v_sq2 = mark("scalar")
                wait("vector", "ss", v_sq2)

                @on("vector")
                def _(e):
                    e.tensor_scalar(out=stat[:, 32:36], in0=rowred[:, 0:4],
                                    scalar1=1.0 / D, scalar2=None, op0=ALU.mult)
                    e.tensor_scalar(out=stat[:, 40:44], in0=stat[:, 36:40],
                                    scalar1=1.0 / D, scalar2=None, op0=ALU.mult)
                    e.tensor_tensor(out=stat[:, 44:48], in0=stat[:, 32:36],
                                    in1=stat[:, 32:36], op=ALU.mult)
                    e.tensor_tensor(out=stat[:, 36:40], in0=stat[:, 40:44],
                                    in1=stat[:, 44:48], op=ALU.subtract)
                    return e.tensor_scalar(out=stat[:, 36:40], in0=stat[:, 36:40],
                                           scalar1=MOE_LN_EPS, scalar2=None, op0=ALU.add)
                v_v2 = mark("vector")
                wait("scalar", "vs", v_v2)

                @on("scalar")
                def _(e):
                    return e.activation(out=small[:, 16:20], in_=stat[:, 36:40],
                                        func=AF.Sqrt)
                v_sq3 = mark("scalar")
                wait("vector", "ss", v_sq3)

                @on("vector")
                def _(e, qq=qq):
                    e.reciprocal(out=stat[:, 40:44], in_=small[:, 16:20])
                    e.tensor_tensor(out=small[:, 0:4], in0=stat[:, 40:44],
                                    in1=stat[:, 40:44], op=ALU.mult)
                    e.tensor_tensor(out=small[:, 0:4], in0=small[:, 0:4],
                                    in1=stat[:, 36:40], op=ALU.mult)
                    e.tensor_scalar(out=small[:, 0:4], in0=small[:, 0:4],
                                    scalar1=-0.5, scalar2=1.5, op0=ALU.mult, op1=ALU.add)
                    e.tensor_tensor(out=stat[:, 40:44], in0=stat[:, 40:44],
                                    in1=small[:, 0:4], op=ALU.mult)
                    for tl in range(4):
                        tt = qq * 4 + tl
                        e.tensor_scalar(out=small[:, 0:D], in0=u32q[:, tl, :],
                                        scalar1=stat[:, 32 + tl:33 + tl],
                                        scalar2=stat[:, 40 + tl:41 + tl],
                                        op0=ALU.subtract, op1=ALU.mult)
                        e.tensor_tensor(out=small[:, 0:D], in0=small[:, 0:D],
                                        in1=hlnw, op=ALU.mult)
                        e.tensor_tensor(out=small[:, 0:D], in0=small[:, 0:D],
                                        in1=hlnb, op=ALU.add)
                        last = e.tensor_tensor(out=x_res[:, tt, :], in0=x_res[:, tt, :],
                                               in1=small[:, 0:D], op=ALU.add)
                    return last
                mark("vector")
            return None

        def sincos(src, dst_sin, dst_cos, t1, t2):
            wait("vector", "ss", sem["ss"].n)

            @on("vector")
            def _(e, src=src, t1=t1, t2=t2):
                e.tensor_scalar(out=t1, in0=src, scalar1=1.0 / (2 * PI),
                                scalar2=None, op0=ALU.mult)
                e.tensor_scalar(out=t2, in0=t1, scalar1=C_RND, scalar2=-C_RND,
                                op0=ALU.add, op1=ALU.add)
                e.tensor_tensor(out=t1, in0=t1, in1=t2, op=ALU.subtract)
                e.tensor_scalar(out=t1, in0=t1, scalar1=2 * PI, scalar2=None,
                                op0=ALU.mult)
                e.tensor_scalar(out=t2, in0=t1, scalar1=-1.0, scalar2=None,
                                op0=ALU.mult)
                e.tensor_tensor(out=t2, in0=t2, in1=t1, op=ALU.max)
                return e.tensor_scalar(out=t2, in0=t2, scalar1=-1.0,
                                       scalar2=PI / 2, op0=ALU.mult, op1=ALU.add)
            v = mark("vector")
            wait("scalar", "vs", v)

            @on("scalar")
            def _(e, t1=t1, t2=t2, dst_sin=dst_sin, dst_cos=dst_cos):
                e.activation(out=dst_sin, in_=t1, func=AF.Sin)
                return e.activation(out=dst_cos, in_=t2, func=AF.Sin)
            return mark("scalar")

        def emit_s4_prestage():
            emit_wprep(Ws4_in[0:D, :], Ws4T_d[:, 0:D], 8)
            emit_wprep(Ws4_in[D:2 * D, :], Ws4T_d[:, D:2 * D], 8)
            v_ws4prep = sem["gl"].n

            sync_load(small[:, 0:32], my_Cre_in)
            sync_load(small[:, 32:64], my_Cim_in)
            sync_load(small[:, 64:96], my_logAre_in)
            sync_load(small[:, 96:128], my_Aim_in)
            sync_load(small[:, 128:129], my_logdt_in)
            v_prm = sync_load(small[:, 129:130], my_D_in)

            lgridv = view(A5, 0, F32, (129,))
            lrevv = view(A5, 1024, F32, (128, 1))

            @on("gpsimd")
            def _(e):
                return e.iota(lgridv, [[1, 129]], channel_multiplier=0,
                              allow_small_or_imprecise_dtypes=True).then_inc(sem["io"].h, 1)
            v_iota = sem["io"].inc()

            wait("scalar", "dl", v_prm)

            @on("scalar")
            def _(e):
                e.activation(out=small[:, 64:96], in_=small[:, 64:96], func=AF.Exp)
                return e.activation(out=small[:, 128:129], in_=small[:, 128:129],
                                    func=AF.Exp)
            v_e1 = mark("scalar")
            wait("vector", "ss", v_e1)

            @on("vector")
            def _(e):
                e.tensor_scalar(out=small[:, 130:131], in0=small[:, 128:129],
                                scalar1=-1.0, scalar2=None, op0=ALU.mult)
                e.tensor_scalar(out=small[:, 132:164], in0=small[:, 64:96],
                                scalar1=small[:, 130:131], scalar2=None, op0=ALU.mult)
                return e.tensor_scalar(out=small[:, 164:196], in0=small[:, 96:128],
                                       scalar1=small[:, 128:129], scalar2=None,
                                       op0=ALU.mult)
            v_dta = mark("vector")
            wait("scalar", "vs", v_dta)

            @on("scalar")
            def _(e):
                return e.activation(out=small[:, 196:228], in_=small[:, 132:164],
                                    func=AF.Exp)
            mark("scalar")
            sincos(small[:, 164:196], small[:, 228:260], small[:, 260:292],
                   small[:, 612:644], small[:, 644:676])
            wait("vector", "ss", sem["ss"].n)

            @on("vector")
            def _(e):
                S = small
                e.tensor_tensor(out=S[:, 260:292], in0=S[:, 260:292], in1=S[:, 196:228], op=ALU.mult)
                e.tensor_tensor(out=S[:, 228:260], in0=S[:, 228:260], in1=S[:, 196:228], op=ALU.mult)
                e.tensor_tensor(out=S[:, 324:356], in0=S[:, 64:96], in1=S[:, 64:96], op=ALU.mult)
                e.tensor_tensor(out=S[:, 356:388], in0=S[:, 96:128], in1=S[:, 96:128], op=ALU.mult)
                e.tensor_tensor(out=S[:, 324:356], in0=S[:, 324:356], in1=S[:, 356:388], op=ALU.add)
                e.reciprocal(out=S[:, 356:388], in_=S[:, 324:356])
                e.tensor_scalar(out=S[:, 388:420], in0=S[:, 260:292], scalar1=-1.0,
                                scalar2=None, op0=ALU.add)
                e.tensor_tensor(out=S[:, 420:452], in0=S[:, 388:420], in1=S[:, 64:96], op=ALU.mult)
                e.tensor_tensor(out=S[:, 452:484], in0=S[:, 228:260], in1=S[:, 96:128], op=ALU.mult)
                e.tensor_tensor(out=S[:, 420:452], in0=S[:, 452:484], in1=S[:, 420:452], op=ALU.subtract)
                e.tensor_tensor(out=S[:, 484:516], in0=S[:, 420:452], in1=S[:, 356:388], op=ALU.mult)
                e.tensor_tensor(out=S[:, 420:452], in0=S[:, 228:260], in1=S[:, 64:96], op=ALU.mult)
                e.tensor_tensor(out=S[:, 452:484], in0=S[:, 388:420], in1=S[:, 96:128], op=ALU.mult)
                e.tensor_tensor(out=S[:, 420:452], in0=S[:, 420:452], in1=S[:, 452:484], op=ALU.add)
                e.tensor_scalar(out=S[:, 420:452], in0=S[:, 420:452], scalar1=-1.0,
                                scalar2=None, op0=ALU.mult)
                e.tensor_tensor(out=S[:, 516:548], in0=S[:, 420:452], in1=S[:, 356:388], op=ALU.mult)
                e.tensor_tensor(out=S[:, 420:452], in0=S[:, 0:32], in1=S[:, 484:516], op=ALU.mult)
                e.tensor_tensor(out=S[:, 452:484], in0=S[:, 32:64], in1=S[:, 516:548], op=ALU.mult)
                e.tensor_tensor(out=S[:, 548:580], in0=S[:, 420:452], in1=S[:, 452:484], op=ALU.subtract)
                e.tensor_tensor(out=S[:, 420:452], in0=S[:, 0:32], in1=S[:, 516:548], op=ALU.mult)
                e.tensor_tensor(out=S[:, 452:484], in0=S[:, 32:64], in1=S[:, 484:516], op=ALU.mult)
                return e.tensor_tensor(out=S[:, 580:612], in0=S[:, 420:452], in1=S[:, 452:484], op=ALU.add)
            mark("vector")
            wait("vector", "io", v_iota)

            POWre = view(A2, 0, F32, (S4_N, 129))
            POWim = view(A3, 0, F32, (S4_N, 129))
            for h in range(2):
                thre = view(A4, 0, F32, (16, 129))
                thre_f = view(A4, 0, F32, (2064,))
                thim = view(A1, 0, F32, (16, 129))
                thim_f = view(A1, 0, F32, (2064,))
                t1f = view(A1, 8256, F32, (2064,))
                t2f = view(A1, 16512, F32, (2064,))
                rhf = view(A5, 2048, F32, (2064,))
                pre_f = view(A2, h * 8256, F32, (2064,))
                pim_f = view(A3, h * 8256, F32, (2064,))
                if h == 0:
                    wait("vector", "gl", v_ws4prep)
                    wait("vector", "ts", sem["ts"].n)
                wait("vector", "ss", sem["ss"].n)

                @on("vector")
                def _(e, h=h, thre=thre, thim=thim):
                    last = None
                    for i in range(16):
                        n = h * 16 + i
                        last = e.tensor_scalar(out=thre[:, i, :], in0=lgridv,
                                               scalar1=small[:, 132 + n:133 + n],
                                               scalar2=None, op0=ALU.mult)
                        last = e.tensor_scalar(out=thim[:, i, :], in0=lgridv,
                                               scalar1=small[:, 164 + n:165 + n],
                                               scalar2=None, op0=ALU.mult)
                    return last
                v_th = mark("vector")
                wait("scalar", "vs", v_th)

                @on("scalar")
                def _(e, thre_f=thre_f, rhf=rhf):
                    return e.activation(out=rhf, in_=thre_f, func=AF.Exp)
                mark("scalar")
                v_sc = sincos(thim_f, pim_f, pre_f, t1f, t2f)
                wait("vector", "ss", v_sc)

                @on("vector")
                def _(e, pre_f=pre_f, pim_f=pim_f, rhf=rhf):
                    e.tensor_tensor(out=pre_f, in0=pre_f, in1=rhf, op=ALU.mult)
                    return e.tensor_tensor(out=pim_f, in0=pim_f, in1=rhf, op=ALU.mult)
                mark("vector")

            Etile = view(A4, 0, BF16, (2 * S4_N, P))
            Etile_f = view(A4, 0, BF16, (8192,))
            t2r = view(A5, 2048, F32, (128,))
            t3r = view(A5, 2560, F32, (128,))
            kacc = view(A5, 3072, F32, (128,))
            kbf = view(A5, 3584, BF16, (256,))
            lcmp_f = view(A5, 4096, F32, (64,))
            lre = view(A5, 4096, F32, (S4_N, 1))
            lim = view(A5, 4224, F32, (S4_N, 1))
            wait("vector", "ss", sem["ss"].n)

            @on("vector")
            def _(e):
                last = None
                for n in range(S4_N):
                    cR = small[:, 548 + n:549 + n]
                    cI = small[:, 580 + n:581 + n]
                    e.tensor_scalar(out=t2r, in0=POWim[:, n, 1:129], scalar1=cI,
                                    scalar2=None, op0=ALU.mult)
                    e.scalar_tensor_tensor(out=t3r, in0=POWre[:, n, 1:129], scalar=cR,
                                           in1=t2r, op0=ALU.mult, op1=ALU.subtract)
                    e.tensor_scalar(out=Etile[:, n, :], in0=t3r, scalar1=2.0,
                                    scalar2=None, op0=ALU.mult)
                    e.tensor_scalar(out=t2r, in0=POWre[:, n, 1:129], scalar1=cI,
                                    scalar2=None, op0=ALU.mult)
                    e.scalar_tensor_tensor(out=t3r, in0=POWim[:, n, 1:129], scalar=cR,
                                           in1=t2r, op0=ALU.mult, op1=ALU.add)
                    last = e.tensor_scalar(out=Etile[:, S4_N + n, :], in0=t3r,
                                           scalar1=-2.0, scalar2=None, op0=ALU.mult)
                return last

            @on("vector")
            def _(e):
                for n in range(S4_N):
                    cR = small[:, 548 + n:549 + n]
                    cI = small[:, 580 + n:581 + n]
                    e.tensor_scalar(out=t2r, in0=POWim[:, n, 0:128], scalar1=cI,
                                    scalar2=None, op0=ALU.mult)
                    e.scalar_tensor_tensor(out=t3r, in0=POWre[:, n, 0:128], scalar=cR,
                                           in1=t2r, op0=ALU.mult, op1=ALU.subtract)
                    if n == 0:
                        e.tensor_copy(out=kacc, in_=t3r)
                    else:
                        e.tensor_tensor(out=kacc, in0=kacc, in1=t3r, op=ALU.add)
                e.tensor_scalar(out=kacc, in0=kacc, scalar1=2.0, scalar2=None, op0=ALU.mult)
                e.tensor_tensor(out=kacc[:, 0:1], in0=kacc[:, 0:1],
                                in1=small[:, 129:130], op=ALU.add)
                e.memset(kbf, 0.0)
                e.tensor_copy(out=kbf[:, 0:128], in_=kacc)
                e.tensor_copy(out=lre, in_=POWre[:, :, 128:129])
                return e.tensor_copy(out=lim, in_=POWim[:, :, 128:129])

            kRep = view(A1, 0, BF16, (64, 256))
            kRep_f = view(A1, 0, BF16, (16384,))

            @on("vector")
            def _(e):
                last = None
                for j in range(64):
                    last = e.tensor_copy(out=kRep[:, j, :], in_=kbf)
                return last
            v_krep = mark("vector")

            wait("gpsimd", "vs", v_krep)
            gl_dma(bass.AP(dramT_d, 0, [[32768, 128], [1, 16384]]), kRep_f)
            gl_dma(bass.AP(dramT_d, 16384, [[32768, 128], [1, 16384]]), kRep_f)
            gl_dma(bass.AP(dramE_d, 0, [[8192, 128], [1, 8192]]), Etile_f)
            v_st_ek = gl_dma(bass.AP(dramL_d, 0, [[64, 128], [1, 64]]), lcmp_f)

            thre_r = view(A4, 0, F32, (128, S4_N))
            thre_rf = view(A4, 0, F32, (4096,))
            thim_r = view(A1, 0, F32, (128, S4_N))
            thim_rf = view(A1, 0, F32, (4096,))
            rR = view(A1, 16384, F32, (128, S4_N))
            rR_f = view(A1, 16384, F32, (4096,))
            sinR = view(A3, 0, F32, (128, S4_N))
            sinR_f = view(A3, 0, F32, (4096,))
            cosR = view(A3, 16384, F32, (128, S4_N))
            cosR_f = view(A3, 16384, F32, (4096,))
            t1Rf = view(A2, 0, F32, (4096,))
            t2Rf = view(A2, 16384, F32, (4096,))
            Ptile = view(A4, 0, BF16, (128, 2 * S4_N))
            Ptile_f = view(A4, 0, BF16, (8192,))
            wait("vector", "gl", v_st_ek)

            @on("vector")
            def _(e):
                e.tensor_scalar(out=lrevv, in0=view(A5, 0, F32, (128, 1)),
                                scalar1=-1.0, scalar2=127.0, op0=ALU.mult, op1=ALU.add)
                last = None
                for n in range(S4_N):
                    e.tensor_scalar(out=thre_r[:, :, n:n + 1], in0=lrevv,
                                    scalar1=small[:, 132 + n:133 + n],
                                    scalar2=None, op0=ALU.mult)
                    last = e.tensor_scalar(out=thim_r[:, :, n:n + 1], in0=lrevv,
                                           scalar1=small[:, 164 + n:165 + n],
                                           scalar2=None, op0=ALU.mult)
                return last
            v_thr = mark("vector")
            wait("scalar", "vs", v_thr)

            @on("scalar")
            def _(e):
                return e.activation(out=rR_f, in_=thre_rf, func=AF.Exp)
            mark("scalar")
            v_sc2 = sincos(thim_rf, sinR_f, cosR_f, t1Rf, t2Rf)
            wait("vector", "ss", v_sc2)

            @on("vector")
            def _(e):
                e.tensor_tensor(out=Ptile[:, :, 0:S4_N], in0=rR, in1=cosR, op=ALU.mult)
                return e.tensor_tensor(out=Ptile[:, :, S4_N:2 * S4_N], in0=rR,
                                       in1=sinR, op=ALU.mult)
            v_ptile = mark("vector")
            wait("gpsimd", "vs", v_ptile)
            v_st_p = gl_dma(bass.AP(dramP_d, 0, [[8192, 128], [1, 8192]]), Ptile_f)

            wait("gpsimd", "gl", v_st_p)

            @on("gpsimd")
            def _(e):
                last = None
                for (src, dst) in ((dramT_d, agT_d), (dramE_d, agE_d),
                                   (dramP_d, agP_d), (dramL_d, agL_d)):
                    last = e.collective_compute(
                        "AllGather", ALU.bypass, replica_groups=[list(range(8))],
                        ins=[src.ap()], outs=[dst.ap()]).then_inc(sem["cc"].h, 1)
                return last
            v_ccS4 = sem["cc"].inc(4)

            return v_ccS4

        def emit_s4(v_ccS4):
            barrier()
            sync_load(reps[:, 0, :], bs4_in[0:D].partition_broadcast(P))
            sync_load(reps[:, 1, :], bs4_in[D:2 * D].partition_broadcast(P))

            wait("vector", "gl", sem["gl"].n)
            wait("scalar", "gl", sem["gl"].n)
            u16v = view(A1, 0, BF16, (NTT, D))
            v_u16 = emit_ln(u16v)

            @on("vector")
            def _(e):
                e.tensor_scalar(out=stat[:, 32:48], in0=stat[:, 16:32], scalar1=1.0,
                                scalar2=None, op0=ALU.add)
                e.tensor_tensor(out=stat[:, 48:64], in0=stat[:, 0:16],
                                in1=stat[:, 16:32], op=ALU.mult)
                return e.tensor_scalar(out=stat[:, 48:64], in0=stat[:, 48:64],
                                       scalar1=-1.0, scalar2=None, op0=ALU.mult)
            v_s12 = mark("vector")

            def stream_ld(eng, pair, dst, src_ap, pre_ts):
                sname = pair[stream_ld.par[pair] % 2]
                stream_ld.par[pair] += 1
                v_pre = sem[sname].n
                if pre_ts is not None:
                    wait(eng, "ts", pre_ts)

                @on(eng)
                def _(e, dst=dst, src_ap=src_ap, v_pre=v_pre, sname=sname):
                    e.wait_ge(sem[sname].h, v_pre)
                    return e.dma_start(out=dst, in_=src_ap).then_inc(sem[sname].h, 16)
                return (sname, sem[sname].inc(16))
            stream_ld.par = {("t0", "t1"): 0, ("g0", "g1"): 0}
            TP = ("t0", "t1")
            GP = ("g0", "g1")

            Pbuf = view(A3, 0, BF16, (2, 16, 2 * S4_N))
            gpk = view(A2, 0, BF16, (D, NTT))
            NGP = 64
            wait("sync", "cc", v_ccS4)
            wait("sync", "vs", v_u16)

            def p_src(g):
                return bass.AP(agP_d, g * 16 * 8192, [[64, 128], [8192, 16], [1, 64]])
            pl = {0: stream_ld("sync", TP, Pbuf[:, 0], p_src(0), None),
                  1: stream_ld("sync", TP, Pbuf[:, 1], p_src(1), None)}
            pmm = {}
            for g in range(NGP):
                b = g % 2
                wait("tensor", pl[g][0], pl[g][1])
                if g < 2:
                    wait("tensor", "vs", v_u16)
                wait_bank("tensor", b)

                @on("tensor")
                def _(e, g=g, b=b):
                    mm = None
                    for i in range(16):
                        c = g * 16 + i
                        mm = e.matmul(pb[b][0:64, i * 16:(i + 1) * 16],
                                      Pbuf[:, g % 2, i, :], u16v[:, :, c:c + 1],
                                      start=(i == 0), stop=(i == 15),
                                      skip_group_check=True)
                    return mm.then_inc(sem["ts"].h, 1)
                pmm[g] = sem["ts"].inc()
                wait("vector", "ts", pmm[g])

                @on("vector")
                def _(e, g=g, b=b):
                    return e.tensor_copy(
                        out=gpk[0:64, g * 16:(g + 1) * 16, :],
                        in_=pb[b][0:64, 0:256].rearrange("p (c q) -> p c q", c=16))
                bank_free[b] = ("vs", mark("vector"))
                if g + 2 < NGP:
                    pl[g + 2] = stream_ld("sync", TP, Pbuf[:, (g + 2) % 2],
                                          p_src(g + 2), pmm[g])

            FA = view(A3, 0, F32, (NTT, 512))
            prev_ev = sem["vs"].n
            for q in range(NTT):
                wait("tensor", "vs", prev_ev)

                @on("tensor")
                def _(e, q=q):
                    mm = None
                    for g in range(8):
                        mm = e.transpose(pt16b[:, g * 64:(g + 1) * 64],
                                         gpk[0:64, g * 128:(g + 1) * 128, q],
                                         ident[0:64, 0:64])
                    return mm.then_inc(sem["ts"].h, 1)
                v_t = sem["ts"].inc()
                wait("vector", "ts", v_t)

                @on("vector")
                def _(e, q=q):
                    return e.tensor_copy(out=FA[:, q, :], in_=pt16b[:, 0:512])
                prev_ev = mark("vector")

            Lc = view(A4, 0, F32, (8, 64))
            Lc_f = view(A4, 0, F32, (512,))
            Lg = view(A4, 2048, F32, (8, 64))
            Lg_f = view(A4, 2048, F32, (512,))
            tA = view(A4, 4096, F32, (8, 64))
            tB = view(A4, 6144, F32, (8, 64))
            tB_f = view(A4, 6144, F32, (512,))
            sT = [view(A4, 8192, F32, (8, 64)), view(A4, 10240, F32, (8, 64))]
            sTf = [view(A4, 8192, F32, (512,)), view(A4, 10240, F32, (512,))]
            s16 = view(A4, 12288, BF16, (8, 64))
            s16_f = view(A4, 12288, BF16, (512,))
            GT = view(A4, 13312, F32, (8, 64))
            GT_f = view(A4, 13312, F32, (512,))
            gath = view(A5, 0, F32, (4, 512))
            stk = view(A2, 0, BF16, (8, P, NTT))

            v_lc = sync_load(Lc_f, bass.AP(agL_d, 0, [[64, 128], [8192, 8], [1, 64]]))
            wait("vector", "dl", v_lc)

            def emit_sq(src, dst):
                @on("vector")
                def _(e, src=src, dst=dst):
                    e.tensor_tensor(out=tA[:, :, 0:32], in0=src[:, :, 0:32],
                                    in1=src[:, :, 0:32], op=ALU.mult)
                    e.tensor_tensor(out=tA[:, :, 32:64], in0=src[:, :, 32:64],
                                    in1=src[:, :, 32:64], op=ALU.mult)
                    e.tensor_tensor(out=tB[:, :, 0:32], in0=src[:, :, 0:32],
                                    in1=src[:, :, 32:64], op=ALU.mult)
                    e.tensor_tensor(out=dst[:, :, 0:32], in0=tA[:, :, 0:32],
                                    in1=tA[:, :, 32:64], op=ALU.subtract)
                    return e.tensor_scalar(out=dst[:, :, 32:64], in0=tB[:, :, 0:32],
                                           scalar1=2.0, scalar2=None, op0=ALU.mult)
            emit_sq(Lc, Lg)
            emit_sq(Lg, GT)
            emit_sq(GT, Lg)
            emit_sq(Lg, GT)

            @on("vector")
            def _(e):
                return e.tensor_copy(out=Lg_f, in_=GT_f)

            def scan_step(src, dst_f, q):
                @on("vector")
                def _(e, src=src, dst_f=dst_f, q=q):
                    e.tensor_tensor(out=tA[:, :, 0:32], in0=src[:, :, 0:32],
                                    in1=Lc[:, :, 0:32], op=ALU.mult)
                    e.tensor_tensor(out=tA[:, :, 32:64], in0=src[:, :, 32:64],
                                    in1=Lc[:, :, 32:64], op=ALU.mult)
                    e.tensor_tensor(out=tB[:, :, 0:32], in0=tA[:, :, 0:32],
                                    in1=tA[:, :, 32:64], op=ALU.subtract)
                    e.tensor_tensor(out=tA[:, :, 0:32], in0=src[:, :, 0:32],
                                    in1=Lc[:, :, 32:64], op=ALU.mult)
                    e.tensor_tensor(out=tA[:, :, 32:64], in0=src[:, :, 32:64],
                                    in1=Lc[:, :, 0:32], op=ALU.mult)
                    e.tensor_tensor(out=tB[:, :, 32:64], in0=tA[:, :, 0:32],
                                    in1=tA[:, :, 32:64], op=ALU.add)
                    return e.tensor_tensor(out=dst_f, in0=tB_f, in1=FA[:, q, :],
                                           op=ALU.add)

            @on("vector")
            def _(e):
                return e.tensor_copy(out=sTf[0], in_=FA[:, 0, :])
            cur = 0
            for q in range(1, NTT):
                scan_step(sT[cur], sTf[1 - cur], q)
                cur = 1 - cur
            v_tloc = mark("vector")
            wait("gpsimd", "vs", v_tloc)
            gl_dma(bass.AP(s4st_d, 0, [[512, 128], [1, 512]]), sTf[cur])
            v_stg = sem["gl"].n
            wait("gpsimd", "gl", v_stg)

            @on("gpsimd")
            def _(e):
                return e.collective_compute(
                    "AllGather", ALU.bypass, replica_groups=[[0, 1, 2, 3], [4, 5, 6, 7]],
                    ins=[s4st_d.ap()], outs=[s4ag_d.ap()]).then_inc(sem["cc"].h, 1)
            v_ccSt = sem["cc"].inc()
            wait("gpsimd", "cc", v_ccSt)
            v_gath = gl_dma(gath, s4ag_d.ap().rearrange("s p c -> p s c"))

            wait("vector", "gl", v_gath)
            G = sT[1 - cur]
            G_f = sTf[1 - cur]

            @on("vector")
            def _(e, G_f=G_f):
                return e.memset(G_f, 0.0)
            for r in range(4):
                @on("vector")
                def _(e, r=r, G=G, G_f=G_f):
                    e.tensor_tensor(out=tA[:, :, 0:32], in0=G[:, :, 0:32],
                                    in1=Lg[:, :, 0:32], op=ALU.mult)
                    e.tensor_tensor(out=tA[:, :, 32:64], in0=G[:, :, 32:64],
                                    in1=Lg[:, :, 32:64], op=ALU.mult)
                    e.tensor_tensor(out=GT[:, :, 0:32], in0=tA[:, :, 0:32],
                                    in1=tA[:, :, 32:64], op=ALU.subtract)
                    e.tensor_tensor(out=tA[:, :, 0:32], in0=G[:, :, 0:32],
                                    in1=Lg[:, :, 32:64], op=ALU.mult)
                    e.tensor_tensor(out=tA[:, :, 32:64], in0=G[:, :, 32:64],
                                    in1=Lg[:, :, 0:32], op=ALU.mult)
                    e.tensor_tensor(out=GT[:, :, 32:64], in0=tA[:, :, 0:32],
                                    in1=tA[:, :, 32:64], op=ALU.add)
                    e.tensor_tensor(out=GT_f, in0=GT_f, in1=gath[:, r, :], op=ALU.add)
                    e.tensor_tensor(out=GT_f, in0=GT_f, in1=G_f, op=ALU.subtract)
                    return e.scalar_tensor_tensor(out=G_f, in0=GT_f,
                                                  scalar=pfx_sb[:, r:r + 1], in1=G_f,
                                                  op0=ALU.mult, op1=ALU.add)

            cur = 1 - cur
            v_tq = sem["ts"].n
            for q in range(NTT):
                wait("vector", "ts", v_tq)

                @on("vector")
                def _(e, cur=cur):
                    return e.tensor_copy(out=s16_f, in_=sTf[cur])
                v_rec = mark("vector")
                wait("tensor", "vs", v_rec)

                @on("tensor")
                def _(e):
                    mm = None
                    for g in range(8):
                        mm = e.transpose(pt16b[0:64, g * 128:(g + 1) * 128],
                                         s16[:, g, :], ident[:])
                    return mm.then_inc(sem["ts"].h, 1)
                v_tq = sem["ts"].inc()
                wait("vector", "ts", v_tq)

                @on("vector")
                def _(e, q=q):
                    return e.tensor_copy(
                        out=stk[0:64, :, :, q:q + 1],
                        in_=pt16b[0:64, :].rearrange("p (g c u) -> p g c u", g=8, u=1))
                mark("vector")
                if q < NTT - 1:
                    scan_step(sT[cur], sTf[1 - cur], q)
                    cur = 1 - cur
            v_scan_end = mark("vector")

            Tbuf = view(A3, 0, BF16, (2, 8, P))
            Ebuf = view(A3, 4096, BF16, (2, 8, P))
            ya = view(A4, 0, BF16, (16, 512))
            ya3 = view(A4, 0, BF16, (16, 32, 16))
            yb = view(A5, 0, BF16, (16, 512))
            yb3 = view(A5, 0, BF16, (16, 32, 16))
            NGC = 128
            wait("sync", "vs", v_scan_end)
            wait("gpsimd", "vs", v_scan_end)
            wait("scalar", "vs", v_scan_end)

            def t_src(g):
                return bass.AP(agT_d, g * 8 * 32768, [[255, 128], [32768, 8], [1, 128]])

            def e_src(g):
                return bass.AP(agE_d, g * 8 * 8192, [[128, 64], [8192, 8], [1, 128]])
            tl = {0: stream_ld("sync", TP, Tbuf[:, 0], t_src(0), None),
                  1: stream_ld("sync", TP, Tbuf[:, 1], t_src(1), None)}
            el = {0: stream_ld("gpsimd", GP, Ebuf[0:64, 0], e_src(0), None),
                  1: stream_ld("gpsimd", GP, Ebuf[0:64, 1], e_src(1), None)}
            cmm = {}
            for gi in range(NGC):
                if gi % 4 == 0:
                    wait_bank("tensor", (gi // 4) % 4)
                wait("tensor", tl[gi][0], tl[gi][1])
                wait("tensor", el[gi][0], el[gi][1])
                if gi < 2:
                    wait("tensor", "vs", v_scan_end)

                @on("tensor")
                def _(e, gi=gi):
                    mm = None
                    for i in range(8):
                        c = gi * 8 + i
                        b = (c // 32) % 4
                        col = (c % 32) * 16
                        mm = e.matmul(pb[b][:, col:col + 16], Tbuf[:, gi % 2, i, :],
                                      u16v[:, :, c:c + 1], start=(c % 32 == 0),
                                      stop=False, skip_group_check=True)
                        mm = e.matmul(pb[b][:, col:col + 16],
                                      Ebuf[0:64, gi % 2, i, :],
                                      stk[0:64, c // 128, c % 128, :],
                                      start=False, stop=(c % 32 == 31),
                                      skip_group_check=True)
                    return mm.then_inc(sem["ts"].h, 1)
                cmm[gi] = sem["ts"].inc()
                if gi % 4 == 3:
                    cb = (gi * 8) // 32
                    wait("scalar", "ts", cmm[gi])

                    @on("scalar")
                    def _(e, cb=cb):
                        tgt = ya if cb < 16 else yb
                        return e.activation(out=tgt[:, cb % 16, :],
                                            in_=pb[cb % 4][:], func=AF.Gelu)
                    bank_free[cb % 4] = ("ss", mark("scalar"))
                if gi + 2 < NGC:
                    tl[gi + 2] = stream_ld("sync", TP, Tbuf[:, gi % 2],
                                           t_src(gi + 2), cmm[gi])
                    el[gi + 2] = stream_ld("gpsimd", GP, Ebuf[0:64, gi % 2],
                                           e_src(gi + 2), cmm[gi])
            v_conv_end = sem["ts"].n
            v_gelu_end = sem["ss"].n

            ws4sb = view(A1, 0, BF16, (8, 2 * D))
            wait("sync", "ts", v_conv_end)
            wait("sync", "gl", sem["gl"].n)
            v_ws4 = sync_load(ws4sb, Ws4T_d.rearrange("(fc p) o -> p fc o", p=P))

            yT = view(A3, 0, BF16, (8, TOK))
            wait("tensor", "ss", v_gelu_end)
            prev_ev = sem["vs"].n
            for q in range(NTT):
                wait("tensor", "vs", prev_ev)

                @on("tensor")
                def _(e, q=q):
                    mm = None
                    for g in range(8):
                        src = ya3 if g < 4 else yb3
                        mm = e.transpose(pt16b[:, g * 128:(g + 1) * 128],
                                         src[:, (g % 4) * 4:(g % 4) * 4 + 4, :, q],
                                         ident[:])
                    return mm.then_inc(sem["ts"].h, 1)
                v_t = sem["ts"].inc()
                wait("vector", "ts", v_t)

                @on("vector")
                def _(e, q=q):
                    return e.tensor_copy(
                        out=yT[:, :, q * P:(q + 1) * P],
                        in_=pt16b[:].rearrange("p (g c) -> p g c", g=8))
                prev_ev = mark("vector")
            v_yT = prev_ev

            glus = view(A4, 0, F32, (D,))
            sgt = view(A4, 4096, F32, (D,))
            sgt2 = view(A4, 8192, F32, (D,))
            wait("tensor", "dl", v_ws4)
            wait("tensor", "vs", v_yT)
            for tb in range(NTT):
                for b in range(4):
                    wait_bank("tensor", b)

                @on("tensor")
                def _(e, tb=tb):
                    mm = None
                    for g in range(8):
                        for ob in range(4):
                            mm = e.matmul(pb[ob][:], yT[:, g, tb * P:(tb + 1) * P],
                                          ws4sb[:, g, ob * 512:(ob + 1) * 512],
                                          start=(g == 0), stop=(g == 7),
                                          skip_group_check=True)
                    return mm.then_inc(sem["ts"].h, 1)
                v_wm = sem["ts"].inc()
                wait("vector", "ts", v_wm)

                @on("vector")
                def _(e):
                    e.tensor_tensor(out=sgt[:, 0:512], in0=pb[2][:],
                                    in1=reps[:, 1, 0:512], op=ALU.add)
                    return e.tensor_tensor(out=sgt[:, 512:1024], in0=pb[3][:],
                                           in1=reps[:, 1, 512:1024], op=ALU.add)
                v_g1 = mark("vector")
                wait("scalar", "vs", v_g1)

                @on("scalar")
                def _(e):
                    return e.activation(out=sgt2, in_=sgt, func=AF.Sigmoid)
                v_g2 = mark("scalar")
                wait("vector", "ss", v_g2)

                @on("vector")
                def _(e, tb=tb):
                    e.tensor_tensor(out=glus[:, 0:512], in0=pb[0][:],
                                    in1=reps[:, 0, 0:512], op=ALU.add)
                    e.tensor_tensor(out=glus[:, 512:1024], in0=pb[1][:],
                                    in1=reps[:, 0, 512:1024], op=ALU.add)
                    e.tensor_tensor(out=glus, in0=glus, in1=sgt2, op=ALU.mult)
                    e.tensor_scalar(out=x_res[:, tb, :], in0=x_res[:, tb, :],
                                    scalar1=stat[:, 32 + tb:33 + tb],
                                    scalar2=stat[:, 48 + tb:49 + tb],
                                    op0=ALU.mult, op1=ALU.add)
                    return e.tensor_tensor(out=x_res[:, tb, :], in0=x_res[:, tb, :],
                                           in1=glus, op=ALU.add)
                v_fin = mark("vector")
                for b in range(4):
                    bank_free[b] = ("vs", v_fin)
            return None

        wait("vector", "dl", sem["dl"].n)
        wait("scalar", "dl", sem["dl"].n)

        v_ccS4 = emit_s4_prestage() if upto >= 5 else 0
        v_ccw = emit_moe_prestage() if upto >= 4 else 0
        if upto >= 1:
            emit_qffn()
        if upto >= 2:
            emit_attention()
        if upto >= 3:
            emit_qffn()
        if upto >= 4:
            emit_moe(v_ccw)
        if upto >= 5:
            emit_s4(v_ccS4)
        if upto >= 6:
            emit_moe(v_ccw)

        v_out_ready = mark("vector")
        wait("gpsimd", "vs", v_out_ready)

        @on("gpsimd")
        def _(e):
            return e.dma_start(out=out_t.rearrange("(tt p) d -> p tt d", p=P),
                               in_=x_res[:]).then_inc(sem["so"].h, 16)
        v_so = sem["so"].inc(16)

        @on("gpsimd")
        def _(e, v=v_so):
            e.wait_ge(sem["so"].h, v)

        block = es.enter_context(nc.Block())

        _NO_DRAIN = {"wait_ge", "wait_op", "nop", "drain", "sem_inc",
                     "dma_start", "dma_start_transpose", "print"}

        class _DrainProxy:

            def __init__(self, e):
                self._e = e

            def __getattr__(self, name):
                attr = getattr(self._e, name)
                if callable(attr) and name not in _NO_DRAIN and not name.startswith("_"):
                    def wrapped(*a, _attr=attr, **k):
                        self._e.drain()
                        return _attr(*a, **k)
                    return wrapped
                return attr

        def _run(e, eng):
            if eng in ("vector", "scalar"):
                e = _DrainProxy(e)
            for f in ops[eng]:
                r = f(e)
                if r is not None:
                    last_inst[eng] = r

        @block.sync
        def _(e):
            _run(e, "sync")

        @block.scalar
        def _(e):
            _run(e, "scalar")

        @block.vector
        def _(e):
            _run(e, "vector")

        @block.tensor
        def _(e):
            _run(e, "tensor")

        @block.gpsimd
        def _(e):
            _run(e, "gpsimd")

    return nc


def make_in_maps(inputs):
    import ml_dtypes
    x = np.asarray(inputs["L0_attn_x"])
    ident16 = np.eye(P, dtype=ml_dtypes.bfloat16)
    shared = {
        "Wq": np.asarray(inputs["L0_attn_Wq"]), "Wk": np.asarray(inputs["L0_attn_Wk"]),
        "Wv": np.asarray(inputs["L0_attn_Wv"]), "Wo": np.asarray(inputs["L0_attn_Wo"]),
        "bo": np.asarray(inputs["L0_attn_bo"]),
        "W8": np.asarray(inputs["L0_ffn_weight_q"]),
        "wscale": np.asarray(inputs["L0_ffn_weight_scale"]),
        "qbias": np.asarray(inputs["L0_ffn_bias"]),
        "Ws4": np.asarray(inputs["L1_attn_W"]), "bs4": np.asarray(inputs["L1_attn_b"]),
        "rw": np.asarray(inputs["L1_ffn_router_w"]), "rb": np.asarray(inputs["L1_ffn_router_b"]),
        "b1": np.asarray(inputs["L1_ffn_b1"]),
        "b2": np.asarray(inputs["L1_ffn_b2"]),
        "lnw": np.asarray(inputs["L1_ffn_ln_w"]), "lnb": np.asarray(inputs["L1_ffn_ln_b"]),
        "ident16": ident16,
        "masku4": np.tile((np.arange(P)[:, None] <= np.arange(P)[None, :]), (1, 4)).astype(ml_dtypes.bfloat16),
        "onesk": np.ones((P, HD), dtype=ml_dtypes.bfloat16),
        "ones1": np.ones((P, 1), dtype=ml_dtypes.bfloat16),
    }
    maps = []
    for core in range(8):
        b, q = core // 4, core % 4
        pfx = np.zeros((P, 4), np.float32)
        pfx[:, :q] = 1.0
        m = dict(shared)
        m["x"] = np.ascontiguousarray(x[b, q * TOK:(q + 1) * TOK, :])
        m["pfx"] = pfx
        m["myw1"] = np.ascontiguousarray(np.asarray(inputs["L1_ffn_w1"])[core])
        m["myw2"] = np.ascontiguousarray(np.asarray(inputs["L1_ffn_w2"])[core])
        cs = slice(core * P, (core + 1) * P)
        m["my_logdt"] = np.ascontiguousarray(
            np.asarray(inputs["L1_attn_log_dt"])[cs].reshape(P, 1))
        m["my_Cre"] = np.ascontiguousarray(np.asarray(inputs["L1_attn_C_re"])[cs])
        m["my_Cim"] = np.ascontiguousarray(np.asarray(inputs["L1_attn_C_im"])[cs])
        m["my_logAre"] = np.ascontiguousarray(np.asarray(inputs["L1_attn_log_A_re"])[cs])
        m["my_Aim"] = np.ascontiguousarray(np.asarray(inputs["L1_attn_A_im"])[cs])
        m["my_D"] = np.ascontiguousarray(
            np.asarray(inputs["L1_attn_D"])[cs].reshape(P, 1))
        maps.append(m)
    return maps


def kernel(**inputs):
    nc = build_nc()
    in_maps = make_in_maps(inputs)
    res = run_bass_kernel_spmd(nc, in_maps, core_ids=list(range(8)))
    outs = [res.results[i]["out"] for i in range(8)]
    full = np.stack([np.concatenate(outs[0:4], axis=0),
                     np.concatenate(outs[4:8], axis=0)])
    return full.astype(np.float32)


# version 05 (final)
# speedup vs baseline: 1.0689x; 1.0689x over previous
import contextlib
import math
import numpy as np

import concourse.bass as bass
import concourse.mybir as mybir
from concourse.bass_utils import run_bass_kernel_spmd

F32 = mybir.dt.float32
BF16 = mybir.dt.bfloat16
U8 = mybir.dt.uint8
U32 = mybir.dt.uint32
I8 = mybir.dt.int8
AF = mybir.ActivationFunctionType
ALU = mybir.AluOpType
AX = mybir.AxisListType.X

P = 128
TOK = 2048
NTT = TOK // P
D = 1024
NFC = D // P
FF = 4096
NEXP = 8
NHEAD = 16
HD = 64
NCH = TOK // P
C_RND = 1.5 * 2**23
LN_EPS = 1e-5
ATTN_EPS = 1e-5
MOE_LN_EPS = 1e-12
QNT_EPS = 1e-6
S4_N = 32
PI = math.pi

ENGS = ("sync", "scalar", "vector", "tensor", "gpsimd")
ESEM = {"sync": "qs", "scalar": "ss", "vector": "vs", "tensor": "ts", "gpsimd": "gs"}


class Sem:
    def __init__(self, es, nc, name):
        self.h = es.enter_context(nc.semaphore(name))
        self.n = 0

    def inc(self, k=1):
        self.n += k
        return self.n


def build_nc(upto=99, no_shared=False):
    nc = bass.Bass("TRN2", target_bir_lowering=False)
    SH = {} if no_shared else {"addr_space": "Shared"}

    def inp(name, shape, dt=F32):
        return nc.dram_tensor(name, list(shape), dt, kind="ExternalInput").ap()

    x_in = inp("x", [TOK, D])
    Wq_in = inp("Wq", [D, D]); Wk_in = inp("Wk", [D, D])
    Wv_in = inp("Wv", [D, D]); Wo_in = inp("Wo", [D, D])
    bo_in = inp("bo", [D])
    W8_in = inp("W8", [D, D], I8)
    wscale_in = inp("wscale", [D]); qbias_in = inp("qbias", [D])
    my_logdt_in = inp("my_logdt", [P, 1])
    my_Cre_in = inp("my_Cre", [P, S4_N]); my_Cim_in = inp("my_Cim", [P, S4_N])
    my_logAre_in = inp("my_logAre", [P, S4_N]); my_Aim_in = inp("my_Aim", [P, S4_N])
    my_D_in = inp("my_D", [P, 1])
    Ws4_in = inp("Ws4", [2 * D, D]); bs4_in = inp("bs4", [2 * D])
    rw_in = inp("rw", [NEXP, D]); rb_in = inp("rb", [NEXP])
    myw1_in = inp("myw1", [FF, D]); b1_in = inp("b1", [NEXP, FF])
    myw2_in = inp("myw2", [D, FF]); b2_in = inp("b2", [NEXP, D])
    lnw_in = inp("lnw", [D]); lnb_in = inp("lnb", [D])
    ident_in = inp("ident16", [P, P], BF16)
    pfx_in = inp("pfx", [P, 4])
    masku4_in = inp("masku4", [P, 512], BF16)
    onesk_in = inp("onesk", [P, HD], BF16)
    ones1_in = inp("ones1", [P, 1], BF16)

    out_t = nc.dram_tensor("out", [TOK, D], F32, kind="ExternalOutput").ap()

    W8T_d = nc.dram_tensor("W8T_d", [D, D], BF16).ap()
    WqT_d = nc.dram_tensor("WqT_d", [D, D], BF16).ap()
    WkT_d = nc.dram_tensor("WkT_d", [D, D], BF16).ap()
    WvT_d = nc.dram_tensor("WvT_d", [D, D], BF16).ap()
    WoT_d = nc.dram_tensor("WoT_d", [D, D], BF16).ap()
    Ws4T_d = nc.dram_tensor("Ws4T_d", [D, 2 * D], BF16).ap()
    w1T_d = nc.dram_tensor("w1T_d", [NEXP, D, FF], BF16, **SH)
    w2T_d = nc.dram_tensor("w2T_d", [NEXP, FF, D], BF16, **SH)
    ccw1_d = nc.dram_tensor("ccw1_d", [D, FF], BF16)
    ccw2_d = nc.dram_tensor("ccw2_d", [FF, D], BF16)
    attn_st_d = nc.dram_tensor("attn_st_d", [P, 520], F32)
    attn_ag_d = nc.dram_tensor("attn_ag_d", [4, P, 520], F32)
    dramT_d = nc.dram_tensor("dramT_d", [P, P, 256], BF16)
    dramE_d = nc.dram_tensor("dramE_d", [P, 2 * S4_N, P], BF16)
    dramP_d = nc.dram_tensor("dramP_d", [P, P, 2 * S4_N], BF16)
    dramL_d = nc.dram_tensor("dramL_d", [P, 2 * S4_N], F32)
    agT_d = nc.dram_tensor("agT_d", [NEXP, P, P, 256], BF16, **SH)
    agE_d = nc.dram_tensor("agE_d", [NEXP, P, 2 * S4_N, P], BF16, **SH)
    agP_d = nc.dram_tensor("agP_d", [NEXP, P, P, 2 * S4_N], BF16, **SH)
    agL_d = nc.dram_tensor("agL_d", [NEXP, P, 2 * S4_N], F32, **SH)
    s4st_d = nc.dram_tensor("s4st_d", [P, 512], F32)
    s4ag_d = nc.dram_tensor("s4ag_d", [4, P, 512], F32)

    es = contextlib.ExitStack()
    with es:
        def sb(name, shape, dt):
            return es.enter_context(nc.sbuf_tensor(name, list(shape), dt))

        def psum(name, shape, dt=F32):
            return es.enter_context(nc.psum_tensor(name, list(shape), dt))

        x_res = sb("x_res", [P, NTT, D], F32)
        A1 = sb("A1", [P, 32768], U8).ap()
        A2 = sb("A2", [P, 32768], U8).ap()
        A3 = sb("A3", [P, 32768], U8).ap()
        A4 = sb("A4", [P, 16384], U8).ap()
        A5 = sb("A5", [P, 16384], U8).ap()
        stat = sb("stat", [P, 64], F32)
        cst = sb("cst", [P, 8], F32)
        ident = sb("ident", [P, P], BF16)
        rowred = sb("rowred", [P, NTT], F32)
        small = sb("small", [P, 1024], F32)
        pfx_sb = sb("pfx_sb", [P, 4], F32)
        reps = sb("reps", [P, 2, D], F32)
        bo16 = sb("bo16", [1, D], BF16)

        def view(arena, byte0, dt, shape):
            nbytes = int(np.prod(shape)) * mybir.dt.size(dt)
            v = arena[:, byte0:byte0 + nbytes].bitcast(dt)
            if len(shape) > 1:
                dims = " ".join(f"d{i}" for i in range(len(shape)))
                kw = {f"d{i}": s for i, s in enumerate(shape)}
                v = v.rearrange(f"p ({dims}) -> p {dims}", **kw)
            return v

        pb = [psum(f"pb{i}", [P, 512], F32) for i in range(5)]
        psS = psum("psS", [P, 512], F32)
        psZ = psum("psZ", [P, 512], F32)
        pt16b = psum("pt16b", [P, 1024], BF16)

        sem = {}
        for name in ("dl", "gl", "sl", "so", "ts", "vs", "ss", "gs", "qs", "cc",
                     "t0", "t1", "g0", "g1", "io"):
            sem[name] = Sem(es, nc, name)

        ops = {k: [] for k in ENGS}
        ops_since_mark = {k: 0 for k in ENGS}
        prev_mark = {k: None for k in ENGS}

        def on(eng, is_wait=False):
            def deco(fn):
                ops[eng].append(fn)
                if not is_wait:
                    ops_since_mark[eng] += 1
                return fn
            return deco

        def wait(eng, s, v):
            if isinstance(s, str):
                s = sem[s]

            @on(eng, is_wait=True)
            def _(e, s=s, v=v):
                e.wait_ge(s.h, v)

        last_inst = {g: None for g in ENGS}
        last_marked = {g: None for g in ENGS}

        def mark(eng):
            if eng == "tensor":
                return sem["ts"].n
            if ops_since_mark[eng] == 0 and prev_mark[eng] is not None:
                return prev_mark[eng]
            s = sem[ESEM[eng]]

            @on(eng, is_wait=True)
            def _(e, s=s, eng=eng):
                li = last_inst[eng]
                if li is not None and eng != "gpsimd" and li is not last_marked[eng]:
                    li.then_inc(s.h, 1)
                    last_marked[eng] = li
                else:
                    e.nop(nofuse=True).then_inc(s.h, 1)
            ops_since_mark[eng] = 0
            prev_mark[eng] = s.inc()
            return prev_mark[eng]

        def sync_load(dst, srcfn):
            v_pre = sem["dl"].n

            @on("sync")
            def _(e, dst=dst, srcfn=srcfn, v_pre=v_pre):
                e.wait_ge(sem["dl"].h, v_pre)
                return e.dma_start(out=dst, in_=srcfn() if callable(srcfn) else srcfn).then_inc(sem["dl"].h, 16)
            return sem["dl"].inc(16)

        def gl_dma(dstfn, srcfn, s="gl"):
            v_pre = sem[s].n

            @on("gpsimd")
            def _(e, v_pre=v_pre, s=s):
                e.wait_ge(sem[s].h, v_pre)
                return e.dma_start(out=dstfn() if callable(dstfn) else dstfn,
                                   in_=srcfn() if callable(srcfn) else srcfn).then_inc(sem[s].h, 16)
            return sem[s].inc(16)

        def barrier(engs=("scalar", "vector", "tensor", "gpsimd")):
            marks = {g: mark(g) for g in engs}
            gl_now = sem["gl"].n
            for g in engs:
                for h in engs:
                    if h != g:
                        wait(g, ESEM[h], marks[h])
                if "gpsimd" in engs and g != "gpsimd" and gl_now:
                    wait(g, "gl", gl_now)

        sync_load(x_res[:], x_in.rearrange("(tt p) d -> p tt d", p=P))
        sync_load(ident[:], ident_in)
        sync_load(pfx_sb[:], pfx_in)
        sync_load(reps[:, 0, :], wscale_in.partition_broadcast(P))
        sync_load(reps[:, 1, :], qbias_in.partition_broadcast(P))
        v_base_loads = sem["dl"].n

        @on("vector")
        def _(e):
            e.memset(cst[:, 0:1], 0.0)
            e.memset(cst[:, 1:2], 1.0)
            return e.memset(cst[:, 2:8], 0.0)

        w8i = view(A3, 0, I8, (8, D))
        w8c = view(A1, 0, BF16, (8, D))
        w8t = view(A2, 0, BF16, (8, D))
        W8T_sb = view(A4, 0, BF16, (8, D))

        sync_load(w8i, W8_in.rearrange("(oc p) f -> p oc f", p=P))
        wait("vector", "dl", sem["dl"].n)

        @on("vector")
        def _(e):
            return e.tensor_copy(out=w8c, in_=w8i)
        v_w8cast = mark("vector")

        wait("tensor", "vs", v_w8cast)
        for oc in range(8):
            @on("tensor")
            def _(e, oc=oc):
                mm = None
                for fc in range(8):
                    mm = e.transpose(pt16b[:, fc * P:(fc + 1) * P],
                                     w8c[:, oc, fc * P:(fc + 1) * P], ident[:])
                return mm.then_inc(sem["ts"].h, 1)
            v_t = sem["ts"].inc()
            wait("vector", "ts", v_t)

            @on("vector")
            def _(e, oc=oc):
                return e.tensor_copy(out=w8t[:, :, oc * P:(oc + 1) * P],
                                     in_=pt16b[:].rearrange("p (fc i) -> p fc i", fc=8))
            v_e = mark("vector")
            wait("tensor", "vs", v_e)
        v_w8t_done = sem["vs"].n

        wait("gpsimd", "vs", v_w8t_done)

        @on("gpsimd")
        def _(e):
            return e.dma_start(out=W8T_d.rearrange("(fc p) o -> p fc o", p=P), in_=w8t).then_inc(sem["gl"].h, 16)
        v_w8t_store = sem["gl"].inc(16)



        def emit_wprep(src_ap, dst_dram, n_oc):
            O = n_oc * P
            wi = view(A3, 0, F32, (n_oc, D))
            wc = view(A1, 0, BF16, (n_oc, D))
            wt = view(A2, 0, BF16, (8, O))

            wait("sync", "vs", sem["vs"].n)
            wait("sync", "ts", sem["ts"].n)
            wait("sync", "gl", sem["gl"].n)

            sync_load(wi, src_ap.rearrange("(oc p) f -> p oc f", p=P))
            wait("vector", "dl", sem["dl"].n)

            @on("vector")
            def _(e):
                return e.tensor_copy(out=wc, in_=wi)
            v_c = mark("vector")
            wait("tensor", "vs", v_c)
            for oc in range(n_oc):
                @on("tensor")
                def _(e, oc=oc):
                    mm = None
                    for fc in range(8):
                        mm = e.transpose(pt16b[:, fc * P:(fc + 1) * P],
                                         wc[:, oc, fc * P:(fc + 1) * P], ident[:])
                    return mm.then_inc(sem["ts"].h, 1)
                v_t = sem["ts"].inc()
                wait("vector", "ts", v_t)

                @on("vector")
                def _(e, oc=oc):
                    return e.tensor_copy(out=wt[:, :, oc * P:(oc + 1) * P],
                                         in_=pt16b[:].rearrange("p (fc i) -> p fc i", fc=8))
                v_e = mark("vector")
                wait("tensor", "vs", v_e)
            v_done = sem["vs"].n
            wait("sync", "vs", v_done)

            @on("sync")
            def _(e):
                return e.dma_start(out=dst_dram.rearrange("(fc p) o -> p fc o", p=P),
                                   in_=wt).then_inc(sem["gl"].h, 16)
            return sem["gl"].inc(16)

        def emit_ln(dst16=None, also_uq=False, eps=LN_EPS):
            v_vec0 = sem["vs"].n
            wait("scalar", "vs", v_vec0)
            for tt in range(NTT):
                @on("vector")
                def _(e, tt=tt):
                    return e.reduce_sum(rowred[:, tt:tt + 1], x_res[:, tt, :], axis=AX)

                @on("scalar")
                def _(e, tt=tt):
                    return e.activation(out=small[:], in_=x_res[:, tt, :], func=AF.Square,
                                        accum_out=stat[:, 32 + tt:33 + tt])
            v_sq = mark("scalar")
            wait("vector", "ss", v_sq)

            @on("vector")
            def _(e):
                e.tensor_scalar(out=stat[:, 0:NTT], in0=rowred[:, 0:NTT],
                                scalar1=1.0 / D, scalar2=None, op0=ALU.mult)
                e.tensor_scalar(out=stat[:, 48:48 + NTT], in0=stat[:, 32:32 + NTT],
                                scalar1=1.0 / D, scalar2=None, op0=ALU.mult)
                e.tensor_tensor(out=small[:, 0:NTT], in0=stat[:, 0:NTT],
                                in1=stat[:, 0:NTT], op=ALU.mult)
                e.tensor_tensor(out=stat[:, 32:32 + NTT], in0=stat[:, 48:48 + NTT],
                                in1=small[:, 0:NTT], op=ALU.subtract)
                return e.tensor_scalar(out=stat[:, 32:32 + NTT], in0=stat[:, 32:32 + NTT],
                                       scalar1=eps, scalar2=None, op0=ALU.add)
            v_var = mark("vector")

            wait("scalar", "vs", v_var)

            @on("scalar")
            def _(e):
                return e.activation(out=small[:, 16:16 + NTT], in_=stat[:, 32:32 + NTT],
                                    func=AF.Sqrt)
            v_sqrt = mark("scalar")
            wait("vector", "ss", v_sqrt)

            @on("vector")
            def _(e):
                e.reciprocal(out=stat[:, 16:16 + NTT], in_=small[:, 16:16 + NTT])
                e.tensor_tensor(out=small[:, 0:NTT], in0=stat[:, 16:16 + NTT],
                                in1=stat[:, 16:16 + NTT], op=ALU.mult)
                e.tensor_tensor(out=small[:, 0:NTT], in0=small[:, 0:NTT],
                                in1=stat[:, 32:32 + NTT], op=ALU.mult)
                e.tensor_scalar(out=small[:, 0:NTT], in0=small[:, 0:NTT],
                                scalar1=-0.5, scalar2=1.5, op0=ALU.mult, op1=ALU.add)
                return e.tensor_tensor(out=stat[:, 16:16 + NTT], in0=stat[:, 16:16 + NTT],
                                       in1=small[:, 0:NTT], op=ALU.mult)
            v_rinv = mark("vector")

            if dst16 is None:
                return v_rinv

            if not also_uq:
                for tt in range(NTT):
                    @on("vector")
                    def _(e, tt=tt):
                        return e.tensor_scalar(out=dst16[:, tt, :], in0=x_res[:, tt, :],
                                               scalar1=stat[:, tt:tt + 1],
                                               scalar2=stat[:, 16 + tt:17 + tt],
                                               op0=ALU.subtract, op1=ALU.mult)
                return mark("vector")

            for tt in range(NTT):
                @on("vector")
                def _(e, tt=tt):
                    e.tensor_scalar(out=small[:], in0=x_res[:, tt, :],
                                    scalar1=stat[:, tt:tt + 1], scalar2=None,
                                    op0=ALU.subtract)
                    return e.reduce_max(rowred[:, tt:tt + 1], small[:], axis=AX,
                                        apply_absolute_value=True)

            @on("vector")
            def _(e):
                e.tensor_tensor(out=small[:, 0:NTT], in0=rowred[:, 0:NTT],
                                in1=stat[:, 16:16 + NTT], op=ALU.mult)
                e.tensor_scalar(out=small[:, 0:NTT], in0=small[:, 0:NTT],
                                scalar1=QNT_EPS, scalar2=None, op0=ALU.max)
                e.reciprocal(out=small[:, 16:16 + NTT], in_=small[:, 0:NTT])
                e.tensor_tensor(out=small[:, 16:16 + NTT], in0=small[:, 16:16 + NTT],
                                in1=stat[:, 16:16 + NTT], op=ALU.mult)
                e.tensor_scalar(out=stat[:, 48:48 + NTT], in0=small[:, 16:16 + NTT],
                                scalar1=127.0, scalar2=None, op0=ALU.mult)
                return e.tensor_scalar(out=stat[:, 32:32 + NTT], in0=small[:, 0:NTT],
                                       scalar1=0.5 / 127.0, scalar2=None, op0=ALU.mult)
            for tt in range(NTT):
                @on("vector")
                def _(e, tt=tt):
                    e.tensor_scalar(out=small[:], in0=x_res[:, tt, :],
                                    scalar1=stat[:, tt:tt + 1],
                                    scalar2=stat[:, 48 + tt:49 + tt],
                                    op0=ALU.subtract, op1=ALU.mult)
                    return e.tensor_scalar(out=dst16[:, tt, :], in0=small[:],
                                           scalar1=C_RND, scalar2=-C_RND,
                                           op0=ALU.add, op1=ALU.add)
            return mark("vector")

        def emit_transpose(src16, dstT, tts=None, pt=None, dst_tt0=0):
            if tts is None:
                tts = range(NTT)
            if pt is None:
                pt = pt16b
            for ti, tt in enumerate(tts):
                @on("tensor")
                def _(e, tt=tt, pt=pt):
                    mm = None
                    for fc in range(8):
                        mm = e.transpose(pt[:, fc * P:(fc + 1) * P],
                                         src16[:, tt, fc * P:(fc + 1) * P], ident[:])
                    return mm.then_inc(sem["ts"].h, 1)
                v_t = sem["ts"].inc()
                wait("vector", "ts", v_t)

                @on("vector")
                def _(e, tt=tt, ti=ti, pt=pt):
                    tloc = dst_tt0 + ti
                    return e.tensor_copy(
                        out=dstT[:, :, tloc * P:(tloc + 1) * P],
                        in_=pt[:].rearrange("p (fc i) -> p fc i", fc=8))
                v_e = mark("vector")
                wait("tensor", "vs", v_e)
            return sem["vs"].n

        bank_free = {}

        def wait_bank(eng, key):
            if key in bank_free:
                s, v = bank_free[key]
                wait(eng, s, v)

        def emit_qffn():
            barrier()
            uq16 = view(A1, 0, BF16, (NTT, D))
            uqT = view(A2, 0, BF16, (8, TOK))
            wait("sync", "ts", sem["ts"].n)
            v_gl_now = sem["gl"].n

            @on("sync")
            def _(e, v=v_gl_now):
                e.wait_ge(sem["gl"].h, v)
                return e.dma_start(out=W8T_sb, in_=W8T_d.rearrange("(fc p) o -> p fc o", p=P)).then_inc(sem["sl"].h, 16)
            v_w8_ld = sem["sl"].inc(16)
            v_uq = emit_ln(uq16, also_uq=True)
            wait("tensor", "vs", v_uq)
            emit_transpose(uq16, uqT)
            wait("tensor", "sl", v_w8_ld)
            for i, (tt, oh) in enumerate([(t, o) for t in range(NTT) for o in range(2)]):
                bank = i % 4
                wait_bank("tensor", bank)

                @on("tensor")
                def _(e, tt=tt, oh=oh, bank=bank):
                    mm = None
                    for fc in range(8):
                        mm = e.matmul(pb[bank][:],
                                      uqT[:, fc, tt * P:(tt + 1) * P],
                                      W8T_sb[:, fc, oh * 512:(oh + 1) * 512],
                                      start=(fc == 0), stop=(fc == 7))
                    return mm.then_inc(sem["ts"].h, 1)
                v_mm = sem["ts"].inc()
                wait("vector", "ts", v_mm)

                @on("vector")
                def _(e, tt=tt, oh=oh, bank=bank):
                    sl = slice(oh * 512, (oh + 1) * 512)
                    e.scalar_tensor_tensor(
                        out=small[:, 0:512], in0=pb[bank][:],
                        scalar=stat[:, 32 + tt:33 + tt],
                        in1=reps[:, 0, sl],
                        op0=ALU.mult, op1=ALU.mult)
                    return e.tensor_tensor(out=x_res[:, tt, sl], in0=small[:, 0:512],
                                           in1=x_res[:, tt, sl], op=ALU.add)
                bank_free[bank] = ("vs", mark("vector"))
            for tt in range(NTT):
                @on("vector")
                def _(e, tt=tt):
                    return e.scalar_tensor_tensor(out=x_res[:, tt, :], in0=reps[:, 1, :],
                                                  scalar=0.5, in1=x_res[:, tt, :],
                                                  op0=ALU.mult, op1=ALU.add)
            return mark("vector")


        def emit_attention():
            for (w_src, w_dst) in [(Wq_in, WqT_d), (Wk_in, WkT_d),
                                   (Wv_in, WvT_d), (Wo_in, WoT_d)]:
                barrier()
                emit_wprep(w_src, w_dst, 8)
            barrier()

            v_tm = view(A3, 0, BF16, (4, 1024))
            kT_tm = view(A3, 8192, BF16, (4, 1024))
            attnT = view(A3, 16384, BF16, (8, P))
            S16 = view(A3, 18432, BF16, (512,))
            S_lo = view(A3, 19456, BF16, (512,))
            S_hi = view(A3, 20480, BF16, (512,))
            zsb = view(A3, 21504, F32, (8,))
            zbl = view(A3, 21536, BF16, (8, HD))
            zbh = view(A3, 22560, BF16, (8, HD))
            rd = view(A3, 23584, F32, (1024,))
            S_in16 = view(A3, 27680, BF16, (520,))
            At = view(A3, 28720, BF16, (512,))
            masku4 = view(A3, 29744, BF16, (512,))
            onesk = view(A3, 30768, BF16, (HD,))
            gatherS = view(A3, 0, F32, (4, 520))
            u16 = view(A1, 0, BF16, (NTT, D))
            uT16q = view(A2, 0, BF16, (8, 512))
            q_fm = view(A2, 8192, BF16, (8, 512))
            k_lo = view(A2, 16384, BF16, (8, 512))
            k_hi = view(A2, 24576, BF16, (8, 512))
            WoT_sb = view(A4, 0, BF16, (8, D))
            Wst = view(A5, 0, BF16, (8, D))

            wait("sync", "vs", sem["vs"].n)
            wait("sync", "ts", sem["ts"].n)

            sync_load(masku4, masku4_in)
            sync_load(onesk, onesk_in)
            sync_load(small[0:1, 0:D], bo_in)
            wait("vector", "dl", sem["dl"].n)

            @on("vector")
            def _(e):
                e.tensor_copy(out=bo16[:], in_=small[0:1, 0:D])
                e.memset(k_lo[64:128, :, :], 0.0)
                e.memset(k_hi[0:64, :, :], 0.0)
                e.memset(S_lo[64:128, :], 0.0)
                e.memset(S_hi[0:64, :], 0.0)
                e.memset(zbl[64:128, :, :], 0.0)
                return e.memset(zbh[0:64, :, :], 0.0)
            mark("vector")

            v_u16 = emit_ln(u16)

            v_gl_pre = sem["gl"].n
            wait("sync", "ts", sem["ts"].n)

            @on("sync")
            def _(e, v=v_gl_pre):
                e.wait_ge(sem["gl"].h, v)
                return e.dma_start(out=WoT_sb, in_=WoT_d.rearrange("(fc p) o -> p fc o", p=P)).then_inc(sem["sl"].h, 16)
            v_wo_ld = sem["sl"].inc(16)

            def load_w(dram):
                wait("sync", "ts", sem["ts"].n)
                wait("sync", "gl", sem["gl"].n)
                return sync_load(Wst, dram.rearrange("(fc p) o -> p fc o", p=P))

            def quarter_transpose(qq):
                wait("tensor", "vs", v_u16)
                return emit_transpose(u16, uT16q, tts=range(qq * 4, qq * 4 + 4))

            def phi_evac(bank, outs):
                @on("vector")
                def _(e, bank=bank):
                    return e.tensor_scalar_min(small[:, 0:512], pb[bank][:], 0.0)
                v1 = mark("vector")
                wait("scalar", "vs", v1)

                @on("scalar")
                def _(e):
                    return e.activation(out=small[:, 512:1024], in_=small[:, 0:512],
                                        func=AF.Exp)
                v2 = mark("scalar")
                wait("vector", "ss", v2)
                for dst, rows in outs:
                    @on("vector")
                    def _(e, dst=dst, rows=rows, bank=bank):
                        return e.scalar_tensor_tensor(
                            out=dst, in0=pb[bank][rows[0]:rows[1], :],
                            scalar=0.0, in1=small[rows[0]:rows[1], 512:1024],
                            op0=ALU.max, op1=ALU.add)
                return mark("vector")

            def proj_fm(v_wld, phi, dual, v_uT):
                for Ec in range(8):
                    bank = Ec % 2
                    wait("tensor", "dl", v_wld)
                    wait("tensor", "vs", v_uT)
                    wait_bank("tensor", bank)

                    @on("tensor")
                    def _(e, Ec=Ec, bank=bank):
                        mm = None
                        for fc in range(8):
                            mm = e.matmul(pb[bank][:], Wst[:, fc, Ec * P:(Ec + 1) * P],
                                          uT16q[:, fc, :], start=(fc == 0), stop=(fc == 7),
                                          skip_group_check=True)
                        return mm.then_inc(sem["ts"].h, 1)
                    v_mm = sem["ts"].inc()
                    wait("vector", "ts", v_mm)
                    if dual:
                        bank_free[bank] = ("vs", phi_evac(bank, [(k_lo[0:64, Ec, :], (0, 64)),
                                                                 (k_hi[64:128, Ec, :], (64, 128))]))
                    else:
                        bank_free[bank] = ("vs", phi_evac(bank, [(q_fm[:, Ec, :], (0, 128))]))
                return max(bank_free[0][1], bank_free[1][1])

            def proj_tm(v_wld, phi, dst, v_uT):
                for tl in range(4):
                    for oh in range(2):
                        bank = (tl * 2 + oh) % 2
                        wait("tensor", "dl", v_wld)
                        wait("tensor", "vs", v_uT)
                        wait_bank("tensor", bank)

                        @on("tensor")
                        def _(e, tl=tl, oh=oh, bank=bank):
                            mm = None
                            for fc in range(8):
                                mm = e.matmul(pb[bank][:], uT16q[:, fc, tl * P:(tl + 1) * P],
                                              Wst[:, fc, oh * 512:(oh + 1) * 512],
                                              start=(fc == 0), stop=(fc == 7),
                                              skip_group_check=True)
                            return mm.then_inc(sem["ts"].h, 1)
                        v_mm = sem["ts"].inc()
                        wait("vector", "ts", v_mm)
                        if phi:
                            bank_free[bank] = ("vs", phi_evac(bank, [(dst[:, tl, oh * 512:(oh + 1) * 512], (0, 128))]))
                        else:
                            @on("vector")
                            def _(e, tl=tl, oh=oh, bank=bank):
                                return e.tensor_copy(out=dst[:, tl, oh * 512:(oh + 1) * 512],
                                                     in_=pb[bank][:])
                            bank_free[bank] = ("vs", mark("vector"))
                return max(bank_free[0][1], bank_free[1][1])

            def supd(c, v_kv, b1):
                cq = c % 4
                wait("tensor", "vs", v_kv)
                @on("tensor")
                def _(e, c=c, cq=cq, b1=b1):
                    mm = None
                    for h in range(NHEAD):
                        par = h % 2
                        h2 = h // 2
                        st = b1 and (c == 0) and (h < 2)
                        sp = (c == 15) and (h >= 14)
                        tp = (0, 64 * par)
                        mm = e.matmul(psS[par * 64:par * 64 + 64, h2 * 64:(h2 + 1) * 64],
                                      kT_tm[:, cq, h * 64:(h + 1) * 64],
                                      v_tm[:, cq, h * 64:(h + 1) * 64],
                                      start=st, stop=sp, tile_position=tp,
                                      skip_group_check=True)
                        mm = e.matmul(psZ[par * 64:par * 64 + 64, h2:h2 + 1],
                                      kT_tm[:, cq, h * 64:(h + 1) * 64],
                                      onesk[:, 0:1],
                                      start=st, stop=sp, tile_position=tp,
                                      skip_group_check=True)
                    return mm.then_inc(sem["ts"].h, 1)
                return sem["ts"].inc()

            for qq in range(4):
                v_uT = quarter_transpose(qq)
                v_wk = load_w(WkT_d)
                v_kT = proj_tm(v_wk, True, kT_tm, v_uT)
                v_wv = load_w(WvT_d)
                v_v = proj_tm(v_wv, False, v_tm, v_uT)
                for cl in range(4):
                    supd(qq * 4 + cl, max(v_kT, v_v), True)

            v_b1 = sem["ts"].n
            wait("vector", "ts", v_b1)

            @on("vector")
            def _(e):
                e.tensor_copy(out=small[:, 0:512], in_=psS[:])
                return e.tensor_copy(out=small[:, 512:520], in_=psZ[:, 0:8])
            v_sev = mark("vector")
            wait("gpsimd", "vs", v_sev)

            @on("gpsimd")
            def _(e):
                return e.dma_start(out=attn_st_d.ap(), in_=small[:, 0:520]).then_inc(sem["gl"].h, 16)
            v_st = sem["gl"].inc(16)

            @on("gpsimd")
            def _(e, v=v_st):
                e.wait_ge(sem["gl"].h, v)
                return e.collective_compute(
                    "AllGather", ALU.bypass,
                    replica_groups=[[0, 1, 2, 3], [4, 5, 6, 7]],
                    ins=[attn_st_d.ap()], outs=[attn_ag_d.ap()],
                ).then_inc(sem["cc"].h, 1)
            v_cc = sem["cc"].inc()

            @on("gpsimd")
            def _(e, v=v_cc):
                e.wait_ge(sem["cc"].h, v)
                return e.dma_start(out=gatherS, in_=attn_ag_d.ap().rearrange("s p c -> p s c")).then_inc(sem["gl"].h, 16)
            v_ga = sem["gl"].inc(16)
            wait("vector", "gl", v_ga)

            @on("vector")
            def _(e):
                e.memset(small[:, 0:520], 0.0)
                for i in range(4):
                    e.drain()
                    e.scalar_tensor_tensor(out=small[:, 0:520], in0=gatherS[:, i, :],
                                           scalar=pfx_sb[:, i:i + 1], in1=small[:, 0:520],
                                           op0=ALU.mult, op1=ALU.add)
                e.drain()
                return e.tensor_copy(out=S_in16, in_=small[:, 0:520])
            v_sin = mark("vector")
            wait("tensor", "vs", v_sin)

            @on("tensor")
            def _(e):
                e.matmul(psS[:], ident[:], S_in16[:, 0:512], start=True, stop=False,
                         skip_group_check=True)
                return e.matmul(psZ[:, 0:8], ident[:], S_in16[:, 512:520], start=True,
                                stop=False, skip_group_check=True).then_inc(sem["ts"].h, 1)
            sem["ts"].inc()


            def chunk_body(c):
                cq = c % 4
                v_prior = sem["ts"].n
                wait("vector", "ts", v_prior)

                @on("vector")
                def _(e):
                    e.tensor_copy(out=S16, in_=psS[:])
                    e.drain()
                    e.tensor_copy(out=S_lo[0:64, :], in_=S16[0:64, :])
                    e.tensor_copy(out=S_hi[64:128, :], in_=S16[64:128, :])
                    e.tensor_copy(out=zsb, in_=psZ[:, 0:8])
                    e.drain()
                    last = None
                    for h2 in range(8):
                        e.tensor_copy(out=zbl[0:64, h2, :],
                                      in_=zsb[0:64, h2:h2 + 1].to_broadcast([64, HD]))
                        last = e.tensor_copy(out=zbh[64:128, h2, :],
                                             in_=zsb[64:128, h2:h2 + 1].to_broadcast([64, HD]))
                    return last
                v_state = mark("vector")
                wait("tensor", "vs", v_state)

                start_seen = set()

                def stflag(bank, par):
                    key = (bank, par)
                    if key in start_seen:
                        return False
                    start_seen.add(key)
                    return True

                for g in range(4):
                    heads = range(4 * g, 4 * g + 4)
                    wait_bank("tensor", 0)

                    @on("tensor")
                    def _(e, heads=heads, cq=cq):
                        mm = None
                        for ih, h in enumerate(heads):
                            kp = k_lo if h % 2 == 0 else k_hi
                            mm = e.matmul(pb[0][:, ih * P:(ih + 1) * P],
                                          kp[:, h // 2, cq * P:(cq + 1) * P],
                                          q_fm[:, h // 2, cq * P:(cq + 1) * P],
                                          start=(ih == 0), stop=(ih == 3),
                                          skip_group_check=True)
                        return mm.then_inc(sem["ts"].h, 1)
                    v_a = sem["ts"].inc()
                    wait("vector", "ts", v_a)

                    @on("vector")
                    def _(e):
                        return e.tensor_tensor(out=At, in0=pb[0][:], in1=masku4,
                                               op=ALU.mult)
                    bank_free[0] = ("vs", mark("vector"))
                    wait_bank("tensor", 0)

                    @on("tensor")
                    def _(e, heads=heads, cq=cq, g=g):
                        mm = None
                        for ih, h in enumerate(heads):
                            par = h % 2
                            h2 = h // 2
                            dbank = 1 if h < 8 else 2
                            obk = 3 if h < 8 else 4
                            col = (h2 % 4) * P
                            tp = (0, 64 * par)
                            rows = slice(par * 64, par * 64 + 64)
                            zb = zbl if par == 0 else zbh
                            Sp = S_lo if par == 0 else S_hi
                            mm = e.matmul(pb[dbank][rows, col:col + P], onesk[:, 0:HD],
                                          At[:, ih * P:(ih + 1) * P],
                                          start=stflag(dbank, par), stop=False,
                                          tile_position=tp, skip_group_check=True)
                            mm = e.matmul(pb[dbank][rows, col:col + P], zb[:, h2, :],
                                          q_fm[:, h2, cq * P:(cq + 1) * P],
                                          start=False, stop=False,
                                          tile_position=tp, skip_group_check=True)
                            mm = e.matmul(pb[obk][rows, col:col + P],
                                          v_tm[:, cq, h * HD:(h + 1) * HD],
                                          At[:, ih * P:(ih + 1) * P],
                                          start=stflag(obk, par), stop=False,
                                          tile_position=tp, skip_group_check=True)
                            mm = e.matmul(pb[obk][rows, col:col + P],
                                          Sp[:, h2 * HD:(h2 + 1) * HD],
                                          q_fm[:, h2, cq * P:(cq + 1) * P],
                                          start=False, stop=False,
                                          tile_position=tp, skip_group_check=True)
                            sp = (c == 15) and (h >= 14)
                            mm = e.matmul(psS[rows, h2 * 64:(h2 + 1) * 64],
                                          kT_tm[:, cq, h * HD:(h + 1) * HD],
                                          v_tm[:, cq, h * HD:(h + 1) * HD],
                                          start=False, stop=sp, tile_position=tp,
                                          skip_group_check=True)
                            mm = e.matmul(psZ[rows, h2:h2 + 1],
                                          kT_tm[:, cq, h * HD:(h + 1) * HD],
                                          onesk[:, 0:1],
                                          start=False, stop=sp, tile_position=tp,
                                          skip_group_check=True)
                        return mm.then_inc(sem["ts"].h, 1)
                    sem["ts"].inc()

                v_groups = sem["ts"].n
                wait("vector", "ts", v_groups)

                @on("vector")
                def _(e):
                    e.tensor_scalar(out=rd[:, 0:512], in0=pb[1][:], scalar1=ATTN_EPS,
                                    scalar2=None, op0=ALU.add)
                    e.drain()
                    e.reciprocal(out=rd[:, 0:512], in_=rd[:, 0:512])
                    e.tensor_scalar(out=rd[:, 512:1024], in0=pb[2][:], scalar1=ATTN_EPS,
                                    scalar2=None, op0=ALU.add)
                    e.drain()
                    e.reciprocal(out=rd[:, 512:1024], in_=rd[:, 512:1024])
                    e.drain()
                    e.tensor_tensor(out=attnT[:, 0:4, :], in0=pb[3][:],
                                    in1=rd[:, 0:512], op=ALU.mult)
                    e.drain()
                    return e.tensor_tensor(out=attnT[:, 4:8, :], in0=pb[4][:],
                                           in1=rd[:, 512:1024], op=ALU.mult)
                v_attnT = mark("vector")
                wait("tensor", "vs", v_attnT)
                wait("tensor", "sl", v_wo_ld)

                for oh in range(2):
                    @on("tensor")
                    def _(e, oh=oh):
                        mm = None
                        for fc in range(8):
                            mm = e.matmul(pb[0][:], attnT[:, fc, :],
                                          WoT_sb[:, fc, oh * 512:(oh + 1) * 512],
                                          start=(fc == 0), stop=False,
                                          skip_group_check=True)
                        mm = e.matmul(pb[0][:], masku4[0:1, 0:P],
                                      bo16[0:1, oh * 512:(oh + 1) * 512],
                                      start=False, stop=True, skip_group_check=True)
                        return mm.then_inc(sem["ts"].h, 1)
                    v_wo = sem["ts"].inc()
                    wait("vector", "ts", v_wo)

                    @on("vector")
                    def _(e, oh=oh, c=c):
                        sl = slice(oh * 512, (oh + 1) * 512)
                        return e.scalar_tensor_tensor(out=x_res[:, c, sl], in0=pb[0][:],
                                                      scalar=0.0, in1=x_res[:, c, sl],
                                                      op0=ALU.bypass, op1=ALU.add)
                    bank_free[0] = ("vs", mark("vector"))
                    wait_bank("tensor", 0)

            for qq in range(4):
                v_uT = quarter_transpose(qq)
                v_wq = load_w(WqT_d)
                v_q = proj_fm(v_wq, True, False, v_uT)
                v_wk = load_w(WkT_d)
                v_kp = proj_fm(v_wk, True, True, v_uT)
                v_kT = proj_tm(v_wk, True, kT_tm, v_uT)
                v_wv = load_w(WvT_d)
                v_v = proj_tm(v_wv, False, v_tm, v_uT)
                wait("tensor", "vs", max(v_q, v_kp, v_kT, v_v))
                for cl in range(4):
                    chunk_body(qq * 4 + cl)
            return None


        def emit_wprep2(src2d, dst2d, n_oc, n_fc):
            wi = view(A3, 0, F32, (n_oc, n_fc * P))
            wc = view(A1, 0, BF16, (n_oc, n_fc * P))
            wt = view(A2, 0, BF16, (n_fc, n_oc * P))
            wait("sync", "vs", sem["vs"].n)
            wait("sync", "ts", sem["ts"].n)
            wait("sync", "gl", sem["gl"].n)

            sync_load(wi, src2d.rearrange("(oc p) f -> p oc f", p=P))
            wait("vector", "dl", sem["dl"].n)

            @on("vector")
            def _(e):
                return e.tensor_copy(out=wc, in_=wi)
            v_c = mark("vector")
            wait("tensor", "vs", v_c)
            for oc in range(n_oc):
                @on("tensor")
                def _(e, oc=oc):
                    mm = None
                    for fc in range(n_fc):
                        mm = e.transpose(pt16b[:, fc * P:(fc + 1) * P],
                                         wc[:, oc, fc * P:(fc + 1) * P], ident[:])
                    return mm.then_inc(sem["ts"].h, 1)
                v_t = sem["ts"].inc()
                wait("vector", "ts", v_t)

                @on("vector")
                def _(e, oc=oc, n_fc=n_fc):
                    return e.tensor_copy(
                        out=wt[:, :, oc * P:(oc + 1) * P],
                        in_=pt16b[:, 0:n_fc * P].rearrange("p (fc i) -> p fc i", fc=n_fc))
                v_e = mark("vector")
                wait("tensor", "vs", v_e)
            wait("gpsimd", "vs", sem["vs"].n)

            @on("gpsimd")
            def _(e):
                return e.dma_start(out=dst2d.rearrange("(fc p) o -> p fc o", p=P),
                                   in_=wt).then_inc(sem["gl"].h, 16)
            return sem["gl"].inc(16)

        def emit_moe_prestage():
            for s in range(8):
                emit_wprep2(myw1_in[s * 512:(s + 1) * 512, :],
                            ccw1_d.ap()[:, s * 512:(s + 1) * 512], 4, 8)
            for s in range(8):
                emit_wprep2(myw2_in[:, s * 512:(s + 1) * 512],
                            ccw2_d.ap()[s * 512:(s + 1) * 512, :], 8, 4)
            v_gl = sem["gl"].n
            wait("gpsimd", "gl", v_gl)

            @on("gpsimd")
            def _(e):
                e.collective_compute(
                    "AllGather", ALU.bypass, replica_groups=[list(range(8))],
                    ins=[ccw1_d.ap()], outs=[w1T_d.ap()]).then_inc(sem["cc"].h, 1)
                return e.collective_compute(
                    "AllGather", ALU.bypass, replica_groups=[list(range(8))],
                    ins=[ccw2_d.ap()], outs=[w2T_d.ap()]).then_inc(sem["cc"].h, 1)
            return sem["cc"].inc(2)

        def emit_moe(v_ccw):
            barrier()
            u16q = view(A3, 0, BF16, (4, D))
            ulo_t = view(A3, 8192, BF16, (4, D))
            hlnw = view(A3, 16384, F32, (D,))
            hlnb = view(A3, 20480, F32, (D,))
            rbrep = view(A3, 24576, F32, (8,))
            logits32 = view(A3, 24608, F32, (4, 8))
            cw32 = view(A3, 24736, F32, (4, 8))
            mx8 = view(A3, 24864, F32, (8,))
            idx8 = view(A3, 24896, U32, (8,))
            p12 = view(A3, 24928, F32, (8,))
            b1col = view(A3, 24960, F32, (32,))
            b1all = view(A3, 30528, F32, (NEXP * 32,))
            b2t = view(A3, 25088, BF16, (D,))
            rw16 = view(A3, 27136, BF16, (D,))
            rwt = view(A3, 29184, BF16, (8, 8))
            rwtlo = view(A3, 29312, BF16, (8, 8))
            cwt = view(A3, 29440, BF16, (512,))
            cw16 = view(A3, 30464, BF16, (4, 8))
            rwlo16 = view(A3, 0, BF16, (D,))
            rwlo32 = view(A3, 4096, F32, (D,))
            uT16q = view(A2, 0, BF16, (8, 512))
            uTlo16q = view(A2, 8192, BF16, (8, 512))
            hT = view(A1, 0, BF16, (32, 512))
            u32q = view(A4, 0, F32, (4, D))
            b2f = view(A4, 0, F32, (2 * D,))
            w1w = [view(A5, 0, BF16, (8, 512)), view(A5, 8192, BF16, (8, 512))]
            w2w = [view(A5, 0, BF16, (4, D)), view(A5, 8192, BF16, (4, D))]
            w2h = [view(A5, 0, BF16, (4, 512)), view(A5, 8192, BF16, (4, 512))]

            wait("gpsimd", "cc", v_ccw)
            wait("sync", "cc", v_ccw)

            wait("sync", "vs", sem["vs"].n)
            wait("sync", "ts", sem["ts"].n)

            sync_load(hlnw, lnw_in.partition_broadcast(P))
            sync_load(hlnb, lnb_in.partition_broadcast(P))
            sync_load(rbrep, rb_in.partition_broadcast(P))
            sync_load(small[0:8, 0:D], rw_in)
            sync_load(b2f[0:8, 0:D], b2_in)
            v_cst = sem["dl"].n
            wait("vector", "dl", v_cst)

            @on("vector")
            def _(e):
                e.tensor_scalar(out=hlnw, in0=hlnw, scalar1=0.5, scalar2=None, op0=ALU.mult)
                e.tensor_scalar(out=hlnb, in0=hlnb, scalar1=0.5, scalar2=None, op0=ALU.mult)
                e.tensor_copy(out=b2t[0:8, :], in_=b2f[0:8, 0:D])
                e.tensor_copy(out=rw16[0:8, :], in_=small[0:8, 0:D])
                e.tensor_copy(out=rwlo32[0:8, :], in_=rw16[0:8, :])
                e.tensor_tensor(out=rwlo32[0:8, :], in0=small[0:8, 0:D],
                                in1=rwlo32[0:8, :], op=ALU.subtract)
                return e.tensor_copy(out=rwlo16[0:8, :], in_=rwlo32[0:8, :])
            v_rw = mark("vector")
            wait("tensor", "vs", v_rw)

            for fc in range(8):
                @on("tensor")
                def _(e, fc=fc):
                    e.transpose(pt16b[:, 0:8], rw16[0:8, fc * P:(fc + 1) * P], ident[0:8, 0:8])
                    return e.transpose(pt16b[:, 8:16], rwlo16[0:8, fc * P:(fc + 1) * P],
                                       ident[0:8, 0:8]).then_inc(sem["ts"].h, 1)
                v_t = sem["ts"].inc()
                wait("vector", "ts", v_t)

                @on("vector")
                def _(e, fc=fc):
                    e.tensor_copy(out=rwt[:, fc, :], in_=pt16b[:, 0:8])
                    return e.tensor_copy(out=rwtlo[:, fc, :], in_=pt16b[:, 8:16])
                v_e = mark("vector")
                wait("tensor", "vs", v_e)

            for ex in range(NEXP):
                wait("sync", "vs", sem["vs"].n)

                v_b = sync_load(small[0:32, 0:P], b1_in[ex, :].rearrange("(a b) -> a b", b=P))
                wait("vector", "dl", v_b)

                @on("vector")
                def _(e):
                    return e.tensor_copy(out=rwlo16[0:32, 0:P], in_=small[0:32, 0:P])
                v_bc = mark("vector")
                wait("tensor", "vs", v_bc)

                @on("tensor")
                def _(e):
                    return e.transpose(pt16b[:, 0:32], rwlo16[0:32, 0:P],
                                       ident[0:32, 0:32]).then_inc(sem["ts"].h, 1)
                v_bt = sem["ts"].inc()
                wait("vector", "ts", v_bt)

                @on("vector")
                def _(e, ex=ex):
                    return e.tensor_copy(out=b1all[:, ex * 32:(ex + 1) * 32],
                                         in_=pt16b[:, 0:32])
                v_be = mark("vector")
                wait("tensor", "vs", v_be)

            emit_ln(None)

            buf_busy = {}

            for qq in range(4):
                wait("vector", "ts", sem["ts"].n)
                for tl in range(4):
                    tt = qq * 4 + tl

                    @on("vector")
                    def _(e, tl=tl, tt=tt):
                        e.tensor_scalar(out=u32q[:, tl, :], in0=x_res[:, tt, :],
                                        scalar1=stat[:, tt:tt + 1],
                                        scalar2=stat[:, 16 + tt:17 + tt],
                                        op0=ALU.subtract, op1=ALU.mult)
                        e.tensor_copy(out=u16q[:, tl, :], in_=u32q[:, tl, :])
                        e.tensor_copy(out=small[:, 0:D], in_=u16q[:, tl, :])
                        return e.tensor_tensor(out=ulo_t[:, tl, :], in0=u32q[:, tl, :],
                                               in1=small[:, 0:D], op=ALU.subtract)
                v_u = mark("vector")
                wait("tensor", "vs", v_u)
                emit_transpose(u16q, uT16q, tts=range(4))
                emit_transpose(ulo_t, uTlo16q, tts=range(4))

                for tl in range(4):
                    wait_bank("tensor", 4)
                    wait("tensor", "dl", v_cst)

                    @on("tensor")
                    def _(e, tl=tl):
                        mm = None
                        for fc in range(8):
                            mm = e.matmul(pb[4][:, 0:8], uT16q[:, fc, tl * P:(tl + 1) * P],
                                          rwt[:, fc, :], start=(fc == 0), stop=False,
                                          skip_group_check=True)
                        for fc in range(8):
                            mm = e.matmul(pb[4][:, 0:8], uT16q[:, fc, tl * P:(tl + 1) * P],
                                          rwtlo[:, fc, :], start=False, stop=False,
                                          skip_group_check=True)
                        for fc in range(8):
                            mm = e.matmul(pb[4][:, 0:8], uTlo16q[:, fc, tl * P:(tl + 1) * P],
                                          rwt[:, fc, :], start=False, stop=(fc == 7),
                                          skip_group_check=True)
                        return mm.then_inc(sem["ts"].h, 1)
                    v_lg = sem["ts"].inc()
                    wait("vector", "ts", v_lg)

                    @on("vector")
                    def _(e, tl=tl):
                        return e.tensor_tensor(out=logits32[:, tl, :], in0=pb[4][:, 0:8],
                                               in1=rbrep, op=ALU.add)
                    bank_free[4] = ("vs", mark("vector"))

                    @on("vector")
                    def _(e, tl=tl):
                        e.max(out=mx8, in_=logits32[:, tl, :])
                        e.max_index(out=idx8, in_max=mx8, in_values=logits32[:, tl, :])
                        return e.tensor_tensor(out=p12[:, 0:1], in0=mx8[:, 0:1],
                                               in1=mx8[:, 1:2], op=ALU.subtract)
                    v_mx = mark("vector")
                    wait("scalar", "vs", v_mx)

                    @on("scalar")
                    def _(e):
                        return e.activation(out=p12[:, 2:3], in_=p12[:, 0:1],
                                            func=AF.Sigmoid)
                    v_sg = mark("scalar")
                    wait("vector", "ss", v_sg)

                    @on("vector")
                    def _(e, tl=tl):
                        e.tensor_scalar(out=p12[:, 3:4], in0=p12[:, 2:3], scalar1=-1.0,
                                        scalar2=1.0, op0=ALU.mult, op1=ALU.add)
                        e.tensor_copy(out=p12[:, 4:6], in_=idx8[:, 0:2])
                        for ex in range(NEXP):
                            e.tensor_scalar(out=p12[:, 6:7], in0=p12[:, 4:5],
                                            scalar1=float(ex), scalar2=None,
                                            op0=ALU.is_equal)
                            e.tensor_scalar(out=p12[:, 7:8], in0=p12[:, 5:6],
                                            scalar1=float(ex), scalar2=None,
                                            op0=ALU.is_equal)
                            e.tensor_tensor(out=p12[:, 6:7], in0=p12[:, 6:7],
                                            in1=p12[:, 2:3], op=ALU.mult)
                            e.tensor_tensor(out=p12[:, 7:8], in0=p12[:, 7:8],
                                            in1=p12[:, 3:4], op=ALU.mult)
                            e.tensor_tensor(out=cw32[:, tl, ex:ex + 1], in0=p12[:, 6:7],
                                            in1=p12[:, 7:8], op=ALU.add)
                        return e.tensor_copy(out=cw16[:, tl, :], in_=cw32[:, tl, :])
                    v_cw = mark("vector")
                    wait("tensor", "vs", v_cw)

                    @on("tensor")
                    def _(e, tl=tl):
                        return e.transpose(pt16b[0:8, 0:P], cw16[:, tl, :],
                                           ident[:]).then_inc(sem["ts"].h, 1)
                    v_ct = sem["ts"].inc()
                    wait("vector", "ts", v_ct)

                    @on("vector")
                    def _(e, tl=tl):
                        return e.tensor_copy(out=cwt[0:8, tl * P:(tl + 1) * P],
                                             in_=pt16b[0:8, 0:P])
                    bank_free["ct"] = ("vs", mark("vector"))
                    wait_bank("tensor", "ct")

                for ex in range(NEXP):
                    wait("scalar", "ts", sem["ts"].n)

                    for wnd in range(8):
                        buf = w1w[wnd % 2]
                        key = wnd % 2
                        if key in buf_busy:
                            wait("sync", "ts", buf_busy[key])

                        v_w = sync_load(buf, w1T_d.ap()[ex].rearrange("(fc p) o -> p fc o", p=P)[:, :, wnd * 512:(wnd + 1) * 512])
                        wait("tensor", "dl", v_w)
                        for f2 in range(4):
                            ffc = wnd * 4 + f2
                            pbank = pb[0] if ffc % 2 == 0 else psS
                            bk = 0 if ffc % 2 == 0 else 5
                            wait_bank("tensor", bk)

                            @on("tensor")
                            def _(e, f2=f2, pbank=pbank, buf=buf):
                                mm = None
                                for fc in range(8):
                                    mm = e.matmul(pbank[:], buf[:, fc, f2 * P:(f2 + 1) * P],
                                                  uT16q[:, fc, :], start=(fc == 0),
                                                  stop=(fc == 7), skip_group_check=True)
                                return mm.then_inc(sem["ts"].h, 1)
                            v_mm = sem["ts"].inc()
                            buf_busy[key] = v_mm
                            wait("scalar", "ts", v_mm)

                            @on("scalar")
                            def _(e, ffc=ffc, pbank=pbank, ex=ex):
                                return e.activation(out=hT[:, ffc, :], in_=pbank[:],
                                                    func=AF.Gelu,
                                                    bias=b1all[:, ex * 32 + ffc:ex * 32 + ffc + 1])
                            bank_free[bk] = ("ss", mark("scalar"))
                    v_hT = sem["ss"].n
                    wait("tensor", "ss", v_hT)

                    for ohp in range(2):
                        for wnd in range(8):
                            buf = w2h[wnd % 2]
                            key = wnd % 2
                            if key in buf_busy:
                                wait("sync", "ts", buf_busy[key])

                            v_w = sync_load(buf, w2T_d.ap()[ex, wnd * 512:(wnd + 1) * 512, ohp * 512:(ohp + 1) * 512].rearrange("(fc p) o -> p fc o", p=P))
                            wait("tensor", "dl", v_w)
                            if wnd == 0:
                                for ip in range(1, 5):
                                    wait_bank("tensor", ip)

                            @on("tensor")
                            def _(e, wnd=wnd, buf=buf):
                                mm = None
                                for f2 in range(4):
                                    ffc = wnd * 4 + f2
                                    for tl in range(4):
                                        mm = e.matmul(
                                            pb[1 + tl][:],
                                            hT[:, ffc, tl * P:(tl + 1) * P],
                                            buf[:, f2, :],
                                            start=(ffc == 0), stop=(ffc == 31),
                                            skip_group_check=True)
                                return mm.then_inc(sem["ts"].h, 1)
                            v_mm = sem["ts"].inc()
                            buf_busy[key] = v_mm
                        wait("vector", "ts", sem["ts"].n)
                        for tl in range(4):
                            @on("vector")
                            def _(e, tl=tl, ohp=ohp, ex=ex):
                                sl = slice(ohp * 512, (ohp + 1) * 512)
                                return e.scalar_tensor_tensor(
                                    out=u32q[:, tl, sl], in0=pb[1 + tl][:],
                                    scalar=cw32[:, tl, ex:ex + 1],
                                    in1=u32q[:, tl, sl],
                                    op0=ALU.mult, op1=ALU.add)
                            bank_free[1 + tl] = ("vs", mark("vector"))

                for tl in range(4):
                    for ip in (3, 4):
                        wait_bank("tensor", ip)

                    @on("tensor")
                    def _(e, tl=tl):
                        e.matmul(pb[4][:], cwt[0:8, tl * P:(tl + 1) * P],
                                 b2t[0:8, 0:512], start=True, stop=True,
                                 skip_group_check=True)
                        mm = e.matmul(pb[3][:], cwt[0:8, tl * P:(tl + 1) * P],
                                      b2t[0:8, 512:1024], start=True, stop=True,
                                      skip_group_check=True)
                        return mm.then_inc(sem["ts"].h, 1)
                    v2 = sem["ts"].inc()
                    wait("vector", "ts", v2)

                    @on("vector")
                    def _(e, tl=tl):
                        e.scalar_tensor_tensor(out=u32q[:, tl, 0:512], in0=pb[4][:],
                                               scalar=0.0, in1=u32q[:, tl, 0:512],
                                               op0=ALU.bypass, op1=ALU.add)
                        return e.scalar_tensor_tensor(out=u32q[:, tl, 512:1024], in0=pb[3][:],
                                                      scalar=0.0, in1=u32q[:, tl, 512:1024],
                                                      op0=ALU.bypass, op1=ALU.add)
                    bank_free[4] = ("vs", mark("vector"))
                    bank_free[3] = bank_free[4]

                wait("scalar", "vs", sem["vs"].n)
                for tl in range(4):
                    @on("vector")
                    def _(e, tl=tl):
                        return e.reduce_sum(rowred[:, tl:tl + 1], u32q[:, tl, :], axis=AX)

                    @on("scalar")
                    def _(e, tl=tl):
                        return e.activation(out=small[:], in_=u32q[:, tl, :],
                                            func=AF.Square,
                                            accum_out=stat[:, 36 + tl:37 + tl])
                v_sq2 = mark("scalar")
                wait("vector", "ss", v_sq2)

                @on("vector")
                def _(e):
                    e.tensor_scalar(out=stat[:, 32:36], in0=rowred[:, 0:4],
                                    scalar1=1.0 / D, scalar2=None, op0=ALU.mult)
                    e.tensor_scalar(out=stat[:, 40:44], in0=stat[:, 36:40],
                                    scalar1=1.0 / D, scalar2=None, op0=ALU.mult)
                    e.tensor_tensor(out=stat[:, 44:48], in0=stat[:, 32:36],
                                    in1=stat[:, 32:36], op=ALU.mult)
                    e.tensor_tensor(out=stat[:, 36:40], in0=stat[:, 40:44],
                                    in1=stat[:, 44:48], op=ALU.subtract)
                    return e.tensor_scalar(out=stat[:, 36:40], in0=stat[:, 36:40],
                                           scalar1=MOE_LN_EPS, scalar2=None, op0=ALU.add)
                v_v2 = mark("vector")
                wait("scalar", "vs", v_v2)

                @on("scalar")
                def _(e):
                    return e.activation(out=small[:, 16:20], in_=stat[:, 36:40],
                                        func=AF.Sqrt)
                v_sq3 = mark("scalar")
                wait("vector", "ss", v_sq3)

                @on("vector")
                def _(e, qq=qq):
                    e.reciprocal(out=stat[:, 40:44], in_=small[:, 16:20])
                    e.tensor_tensor(out=small[:, 0:4], in0=stat[:, 40:44],
                                    in1=stat[:, 40:44], op=ALU.mult)
                    e.tensor_tensor(out=small[:, 0:4], in0=small[:, 0:4],
                                    in1=stat[:, 36:40], op=ALU.mult)
                    e.tensor_scalar(out=small[:, 0:4], in0=small[:, 0:4],
                                    scalar1=-0.5, scalar2=1.5, op0=ALU.mult, op1=ALU.add)
                    e.tensor_tensor(out=stat[:, 40:44], in0=stat[:, 40:44],
                                    in1=small[:, 0:4], op=ALU.mult)
                    for tl in range(4):
                        tt = qq * 4 + tl
                        e.tensor_scalar(out=small[:, 0:D], in0=u32q[:, tl, :],
                                        scalar1=stat[:, 32 + tl:33 + tl],
                                        scalar2=stat[:, 40 + tl:41 + tl],
                                        op0=ALU.subtract, op1=ALU.mult)
                        e.tensor_tensor(out=small[:, 0:D], in0=small[:, 0:D],
                                        in1=hlnw, op=ALU.mult)
                        e.tensor_tensor(out=small[:, 0:D], in0=small[:, 0:D],
                                        in1=hlnb, op=ALU.add)
                        last = e.tensor_tensor(out=x_res[:, tt, :], in0=x_res[:, tt, :],
                                               in1=small[:, 0:D], op=ALU.add)
                    return last
                mark("vector")
            return None

        def sincos(src, dst_sin, dst_cos, t1, t2):
            wait("vector", "ss", sem["ss"].n)

            @on("vector")
            def _(e, src=src, t1=t1, t2=t2):
                e.tensor_scalar(out=t1, in0=src, scalar1=1.0 / (2 * PI),
                                scalar2=None, op0=ALU.mult)
                e.tensor_scalar(out=t2, in0=t1, scalar1=C_RND, scalar2=-C_RND,
                                op0=ALU.add, op1=ALU.add)
                e.tensor_tensor(out=t1, in0=t1, in1=t2, op=ALU.subtract)
                e.tensor_scalar(out=t1, in0=t1, scalar1=2 * PI, scalar2=None,
                                op0=ALU.mult)
                e.tensor_scalar(out=t2, in0=t1, scalar1=-1.0, scalar2=None,
                                op0=ALU.mult)
                e.tensor_tensor(out=t2, in0=t2, in1=t1, op=ALU.max)
                return e.tensor_scalar(out=t2, in0=t2, scalar1=-1.0,
                                       scalar2=PI / 2, op0=ALU.mult, op1=ALU.add)
            v = mark("vector")
            wait("scalar", "vs", v)

            @on("scalar")
            def _(e, t1=t1, t2=t2, dst_sin=dst_sin, dst_cos=dst_cos):
                e.activation(out=dst_sin, in_=t1, func=AF.Sin)
                return e.activation(out=dst_cos, in_=t2, func=AF.Sin)
            return mark("scalar")

        def emit_s4_prestage():
            emit_wprep(Ws4_in[0:D, :], Ws4T_d[:, 0:D], 8)
            emit_wprep(Ws4_in[D:2 * D, :], Ws4T_d[:, D:2 * D], 8)
            v_ws4prep = sem["gl"].n

            sync_load(small[:, 0:32], my_Cre_in)
            sync_load(small[:, 32:64], my_Cim_in)
            sync_load(small[:, 64:96], my_logAre_in)
            sync_load(small[:, 96:128], my_Aim_in)
            sync_load(small[:, 128:129], my_logdt_in)
            v_prm = sync_load(small[:, 129:130], my_D_in)

            lgridv = view(A5, 0, F32, (129,))
            lrevv = view(A5, 1024, F32, (128, 1))

            @on("gpsimd")
            def _(e):
                return e.iota(lgridv, [[1, 129]], channel_multiplier=0,
                              allow_small_or_imprecise_dtypes=True).then_inc(sem["io"].h, 1)
            v_iota = sem["io"].inc()

            wait("scalar", "dl", v_prm)

            @on("scalar")
            def _(e):
                e.activation(out=small[:, 64:96], in_=small[:, 64:96], func=AF.Exp)
                return e.activation(out=small[:, 128:129], in_=small[:, 128:129],
                                    func=AF.Exp)
            v_e1 = mark("scalar")
            wait("vector", "ss", v_e1)

            @on("vector")
            def _(e):
                e.tensor_scalar(out=small[:, 130:131], in0=small[:, 128:129],
                                scalar1=-1.0, scalar2=None, op0=ALU.mult)
                e.tensor_scalar(out=small[:, 132:164], in0=small[:, 64:96],
                                scalar1=small[:, 130:131], scalar2=None, op0=ALU.mult)
                return e.tensor_scalar(out=small[:, 164:196], in0=small[:, 96:128],
                                       scalar1=small[:, 128:129], scalar2=None,
                                       op0=ALU.mult)
            v_dta = mark("vector")
            wait("scalar", "vs", v_dta)

            @on("scalar")
            def _(e):
                return e.activation(out=small[:, 196:228], in_=small[:, 132:164],
                                    func=AF.Exp)
            mark("scalar")
            sincos(small[:, 164:196], small[:, 228:260], small[:, 260:292],
                   small[:, 612:644], small[:, 644:676])
            wait("vector", "ss", sem["ss"].n)

            @on("vector")
            def _(e):
                S = small
                e.tensor_tensor(out=S[:, 260:292], in0=S[:, 260:292], in1=S[:, 196:228], op=ALU.mult)
                e.tensor_tensor(out=S[:, 228:260], in0=S[:, 228:260], in1=S[:, 196:228], op=ALU.mult)
                e.tensor_tensor(out=S[:, 324:356], in0=S[:, 64:96], in1=S[:, 64:96], op=ALU.mult)
                e.tensor_tensor(out=S[:, 356:388], in0=S[:, 96:128], in1=S[:, 96:128], op=ALU.mult)
                e.tensor_tensor(out=S[:, 324:356], in0=S[:, 324:356], in1=S[:, 356:388], op=ALU.add)
                e.reciprocal(out=S[:, 356:388], in_=S[:, 324:356])
                e.tensor_scalar(out=S[:, 388:420], in0=S[:, 260:292], scalar1=-1.0,
                                scalar2=None, op0=ALU.add)
                e.tensor_tensor(out=S[:, 420:452], in0=S[:, 388:420], in1=S[:, 64:96], op=ALU.mult)
                e.tensor_tensor(out=S[:, 452:484], in0=S[:, 228:260], in1=S[:, 96:128], op=ALU.mult)
                e.tensor_tensor(out=S[:, 420:452], in0=S[:, 452:484], in1=S[:, 420:452], op=ALU.subtract)
                e.tensor_tensor(out=S[:, 484:516], in0=S[:, 420:452], in1=S[:, 356:388], op=ALU.mult)
                e.tensor_tensor(out=S[:, 420:452], in0=S[:, 228:260], in1=S[:, 64:96], op=ALU.mult)
                e.tensor_tensor(out=S[:, 452:484], in0=S[:, 388:420], in1=S[:, 96:128], op=ALU.mult)
                e.tensor_tensor(out=S[:, 420:452], in0=S[:, 420:452], in1=S[:, 452:484], op=ALU.add)
                e.tensor_scalar(out=S[:, 420:452], in0=S[:, 420:452], scalar1=-1.0,
                                scalar2=None, op0=ALU.mult)
                e.tensor_tensor(out=S[:, 516:548], in0=S[:, 420:452], in1=S[:, 356:388], op=ALU.mult)
                e.tensor_tensor(out=S[:, 420:452], in0=S[:, 0:32], in1=S[:, 484:516], op=ALU.mult)
                e.tensor_tensor(out=S[:, 452:484], in0=S[:, 32:64], in1=S[:, 516:548], op=ALU.mult)
                e.tensor_tensor(out=S[:, 548:580], in0=S[:, 420:452], in1=S[:, 452:484], op=ALU.subtract)
                e.tensor_tensor(out=S[:, 420:452], in0=S[:, 0:32], in1=S[:, 516:548], op=ALU.mult)
                e.tensor_tensor(out=S[:, 452:484], in0=S[:, 32:64], in1=S[:, 484:516], op=ALU.mult)
                return e.tensor_tensor(out=S[:, 580:612], in0=S[:, 420:452], in1=S[:, 452:484], op=ALU.add)
            mark("vector")
            wait("vector", "io", v_iota)

            POWre = view(A2, 0, F32, (S4_N, 129))
            POWim = view(A3, 0, F32, (S4_N, 129))
            for h in range(2):
                thre = view(A4, 0, F32, (16, 129))
                thre_f = view(A4, 0, F32, (2064,))
                thim = view(A1, 0, F32, (16, 129))
                thim_f = view(A1, 0, F32, (2064,))
                t1f = view(A1, 8256, F32, (2064,))
                t2f = view(A1, 16512, F32, (2064,))
                rhf = view(A5, 2048, F32, (2064,))
                pre_f = view(A2, h * 8256, F32, (2064,))
                pim_f = view(A3, h * 8256, F32, (2064,))
                if h == 0:
                    wait("vector", "gl", v_ws4prep)
                    wait("vector", "ts", sem["ts"].n)
                wait("vector", "ss", sem["ss"].n)

                @on("vector")
                def _(e, h=h, thre=thre, thim=thim):
                    last = None
                    for i in range(16):
                        n = h * 16 + i
                        last = e.tensor_scalar(out=thre[:, i, :], in0=lgridv,
                                               scalar1=small[:, 132 + n:133 + n],
                                               scalar2=None, op0=ALU.mult)
                        last = e.tensor_scalar(out=thim[:, i, :], in0=lgridv,
                                               scalar1=small[:, 164 + n:165 + n],
                                               scalar2=None, op0=ALU.mult)
                    return last
                v_th = mark("vector")
                wait("scalar", "vs", v_th)

                @on("scalar")
                def _(e, thre_f=thre_f, rhf=rhf):
                    return e.activation(out=rhf, in_=thre_f, func=AF.Exp)
                mark("scalar")
                v_sc = sincos(thim_f, pim_f, pre_f, t1f, t2f)
                wait("vector", "ss", v_sc)

                @on("vector")
                def _(e, pre_f=pre_f, pim_f=pim_f, rhf=rhf):
                    e.tensor_tensor(out=pre_f, in0=pre_f, in1=rhf, op=ALU.mult)
                    return e.tensor_tensor(out=pim_f, in0=pim_f, in1=rhf, op=ALU.mult)
                mark("vector")

            Etile = view(A4, 0, BF16, (2 * S4_N, P))
            Etile_f = view(A4, 0, BF16, (8192,))
            t2r = view(A5, 2048, F32, (128,))
            t3r = view(A5, 2560, F32, (128,))
            kacc = view(A5, 3072, F32, (128,))
            kbf = view(A5, 3584, BF16, (256,))
            lcmp_f = view(A5, 4096, F32, (64,))
            lre = view(A5, 4096, F32, (S4_N, 1))
            lim = view(A5, 4224, F32, (S4_N, 1))
            wait("vector", "ss", sem["ss"].n)

            @on("vector")
            def _(e):
                last = None
                for n in range(S4_N):
                    cR = small[:, 548 + n:549 + n]
                    cI = small[:, 580 + n:581 + n]
                    e.tensor_scalar(out=t2r, in0=POWim[:, n, 1:129], scalar1=cI,
                                    scalar2=None, op0=ALU.mult)
                    e.scalar_tensor_tensor(out=t3r, in0=POWre[:, n, 1:129], scalar=cR,
                                           in1=t2r, op0=ALU.mult, op1=ALU.subtract)
                    e.tensor_scalar(out=Etile[:, n, :], in0=t3r, scalar1=2.0,
                                    scalar2=None, op0=ALU.mult)
                    e.tensor_scalar(out=t2r, in0=POWre[:, n, 1:129], scalar1=cI,
                                    scalar2=None, op0=ALU.mult)
                    e.scalar_tensor_tensor(out=t3r, in0=POWim[:, n, 1:129], scalar=cR,
                                           in1=t2r, op0=ALU.mult, op1=ALU.add)
                    last = e.tensor_scalar(out=Etile[:, S4_N + n, :], in0=t3r,
                                           scalar1=-2.0, scalar2=None, op0=ALU.mult)
                return last

            @on("vector")
            def _(e):
                for n in range(S4_N):
                    cR = small[:, 548 + n:549 + n]
                    cI = small[:, 580 + n:581 + n]
                    e.tensor_scalar(out=t2r, in0=POWim[:, n, 0:128], scalar1=cI,
                                    scalar2=None, op0=ALU.mult)
                    e.scalar_tensor_tensor(out=t3r, in0=POWre[:, n, 0:128], scalar=cR,
                                           in1=t2r, op0=ALU.mult, op1=ALU.subtract)
                    if n == 0:
                        e.tensor_copy(out=kacc, in_=t3r)
                    else:
                        e.tensor_tensor(out=kacc, in0=kacc, in1=t3r, op=ALU.add)
                e.tensor_scalar(out=kacc, in0=kacc, scalar1=2.0, scalar2=None, op0=ALU.mult)
                e.tensor_tensor(out=kacc[:, 0:1], in0=kacc[:, 0:1],
                                in1=small[:, 129:130], op=ALU.add)
                e.memset(kbf, 0.0)
                e.tensor_copy(out=kbf[:, 0:128], in_=kacc)
                e.tensor_copy(out=lre, in_=POWre[:, :, 128:129])
                return e.tensor_copy(out=lim, in_=POWim[:, :, 128:129])

            kRep = view(A1, 0, BF16, (64, 256))
            kRep_f = view(A1, 0, BF16, (16384,))

            @on("vector")
            def _(e):
                last = None
                for j in range(64):
                    last = e.tensor_copy(out=kRep[:, j, :], in_=kbf)
                return last
            v_krep = mark("vector")

            wait("gpsimd", "vs", v_krep)
            gl_dma(bass.AP(dramT_d, 0, [[32768, 128], [1, 16384]]), kRep_f)
            gl_dma(bass.AP(dramT_d, 16384, [[32768, 128], [1, 16384]]), kRep_f)
            gl_dma(bass.AP(dramE_d, 0, [[8192, 128], [1, 8192]]), Etile_f)
            v_st_ek = gl_dma(bass.AP(dramL_d, 0, [[64, 128], [1, 64]]), lcmp_f)

            thre_r = view(A4, 0, F32, (128, S4_N))
            thre_rf = view(A4, 0, F32, (4096,))
            thim_r = view(A1, 0, F32, (128, S4_N))
            thim_rf = view(A1, 0, F32, (4096,))
            rR = view(A1, 16384, F32, (128, S4_N))
            rR_f = view(A1, 16384, F32, (4096,))
            sinR = view(A3, 0, F32, (128, S4_N))
            sinR_f = view(A3, 0, F32, (4096,))
            cosR = view(A3, 16384, F32, (128, S4_N))
            cosR_f = view(A3, 16384, F32, (4096,))
            t1Rf = view(A2, 0, F32, (4096,))
            t2Rf = view(A2, 16384, F32, (4096,))
            Ptile = view(A4, 0, BF16, (128, 2 * S4_N))
            Ptile_f = view(A4, 0, BF16, (8192,))
            wait("vector", "gl", v_st_ek)

            @on("vector")
            def _(e):
                e.tensor_scalar(out=lrevv, in0=view(A5, 0, F32, (128, 1)),
                                scalar1=-1.0, scalar2=127.0, op0=ALU.mult, op1=ALU.add)
                last = None
                for n in range(S4_N):
                    e.tensor_scalar(out=thre_r[:, :, n:n + 1], in0=lrevv,
                                    scalar1=small[:, 132 + n:133 + n],
                                    scalar2=None, op0=ALU.mult)
                    last = e.tensor_scalar(out=thim_r[:, :, n:n + 1], in0=lrevv,
                                           scalar1=small[:, 164 + n:165 + n],
                                           scalar2=None, op0=ALU.mult)
                return last
            v_thr = mark("vector")
            wait("scalar", "vs", v_thr)

            @on("scalar")
            def _(e):
                return e.activation(out=rR_f, in_=thre_rf, func=AF.Exp)
            mark("scalar")
            v_sc2 = sincos(thim_rf, sinR_f, cosR_f, t1Rf, t2Rf)
            wait("vector", "ss", v_sc2)

            @on("vector")
            def _(e):
                e.tensor_tensor(out=Ptile[:, :, 0:S4_N], in0=rR, in1=cosR, op=ALU.mult)
                return e.tensor_tensor(out=Ptile[:, :, S4_N:2 * S4_N], in0=rR,
                                       in1=sinR, op=ALU.mult)
            v_ptile = mark("vector")
            wait("gpsimd", "vs", v_ptile)
            v_st_p = gl_dma(bass.AP(dramP_d, 0, [[8192, 128], [1, 8192]]), Ptile_f)

            wait("gpsimd", "gl", v_st_p)

            @on("gpsimd")
            def _(e):
                last = None
                for (src, dst) in ((dramT_d, agT_d), (dramE_d, agE_d),
                                   (dramP_d, agP_d), (dramL_d, agL_d)):
                    last = e.collective_compute(
                        "AllGather", ALU.bypass, replica_groups=[list(range(8))],
                        ins=[src.ap()], outs=[dst.ap()]).then_inc(sem["cc"].h, 1)
                return last
            v_ccS4 = sem["cc"].inc(4)

            return v_ccS4

        def emit_s4(v_ccS4):
            barrier()
            sync_load(reps[:, 0, :], bs4_in[0:D].partition_broadcast(P))
            sync_load(reps[:, 1, :], bs4_in[D:2 * D].partition_broadcast(P))

            wait("vector", "gl", sem["gl"].n)
            wait("scalar", "gl", sem["gl"].n)
            u16v = view(A1, 0, BF16, (NTT, D))
            v_u16 = emit_ln(u16v)

            @on("vector")
            def _(e):
                e.tensor_scalar(out=stat[:, 32:48], in0=stat[:, 16:32], scalar1=1.0,
                                scalar2=None, op0=ALU.add)
                e.tensor_tensor(out=stat[:, 48:64], in0=stat[:, 0:16],
                                in1=stat[:, 16:32], op=ALU.mult)
                return e.tensor_scalar(out=stat[:, 48:64], in0=stat[:, 48:64],
                                       scalar1=-1.0, scalar2=None, op0=ALU.mult)
            v_s12 = mark("vector")

            def stream_ld(eng, pair, dst, src_ap, pre_ts):
                sname = pair[stream_ld.par[pair] % 2]
                stream_ld.par[pair] += 1
                v_pre = sem[sname].n
                if pre_ts is not None:
                    wait(eng, "ts", pre_ts)

                @on(eng)
                def _(e, dst=dst, src_ap=src_ap, v_pre=v_pre, sname=sname):
                    e.wait_ge(sem[sname].h, v_pre)
                    return e.dma_start(out=dst, in_=src_ap).then_inc(sem[sname].h, 16)
                return (sname, sem[sname].inc(16))
            stream_ld.par = {("t0", "t1"): 0, ("g0", "g1"): 0}
            TP = ("t0", "t1")
            GP = ("g0", "g1")

            Pbuf = view(A3, 0, BF16, (2, 16, 2 * S4_N))
            gpk = view(A2, 0, BF16, (D, NTT))
            NGP = 64
            wait("sync", "cc", v_ccS4)
            wait("sync", "vs", v_u16)

            def p_src(g):
                return bass.AP(agP_d, g * 16 * 8192, [[64, 128], [8192, 16], [1, 64]])
            pl = {0: stream_ld("sync", TP, Pbuf[:, 0], p_src(0), None),
                  1: stream_ld("sync", TP, Pbuf[:, 1], p_src(1), None)}
            pmm = {}
            for g in range(NGP):
                b = g % 2
                wait("tensor", pl[g][0], pl[g][1])
                if g < 2:
                    wait("tensor", "vs", v_u16)
                wait_bank("tensor", b)

                @on("tensor")
                def _(e, g=g, b=b):
                    mm = None
                    for i in range(16):
                        c = g * 16 + i
                        mm = e.matmul(pb[b][0:64, i * 16:(i + 1) * 16],
                                      Pbuf[:, g % 2, i, :], u16v[:, :, c:c + 1],
                                      start=(i == 0), stop=(i == 15),
                                      skip_group_check=True)
                    return mm.then_inc(sem["ts"].h, 1)
                pmm[g] = sem["ts"].inc()
                wait("vector", "ts", pmm[g])

                @on("vector")
                def _(e, g=g, b=b):
                    return e.tensor_copy(
                        out=gpk[0:64, g * 16:(g + 1) * 16, :],
                        in_=pb[b][0:64, 0:256].rearrange("p (c q) -> p c q", c=16))
                bank_free[b] = ("vs", mark("vector"))
                if g + 2 < NGP:
                    pl[g + 2] = stream_ld("sync", TP, Pbuf[:, (g + 2) % 2],
                                          p_src(g + 2), pmm[g])

            FA = view(A3, 0, F32, (NTT, 512))
            prev_ev = sem["vs"].n
            for q in range(NTT):
                wait("tensor", "vs", prev_ev)

                @on("tensor")
                def _(e, q=q):
                    mm = None
                    for g in range(8):
                        mm = e.transpose(pt16b[:, g * 64:(g + 1) * 64],
                                         gpk[0:64, g * 128:(g + 1) * 128, q],
                                         ident[0:64, 0:64])
                    return mm.then_inc(sem["ts"].h, 1)
                v_t = sem["ts"].inc()
                wait("vector", "ts", v_t)

                @on("vector")
                def _(e, q=q):
                    return e.tensor_copy(out=FA[:, q, :], in_=pt16b[:, 0:512])
                prev_ev = mark("vector")

            Lc = view(A4, 0, F32, (8, 64))
            Lc_f = view(A4, 0, F32, (512,))
            Lg = view(A4, 2048, F32, (8, 64))
            Lg_f = view(A4, 2048, F32, (512,))
            tA = view(A4, 4096, F32, (8, 64))
            tB = view(A4, 6144, F32, (8, 64))
            tB_f = view(A4, 6144, F32, (512,))
            sT = [view(A4, 8192, F32, (8, 64)), view(A4, 10240, F32, (8, 64))]
            sTf = [view(A4, 8192, F32, (512,)), view(A4, 10240, F32, (512,))]
            s16 = view(A4, 12288, BF16, (8, 64))
            s16_f = view(A4, 12288, BF16, (512,))
            GT = view(A4, 13312, F32, (8, 64))
            GT_f = view(A4, 13312, F32, (512,))
            gath = view(A5, 0, F32, (4, 512))
            stk = view(A2, 0, BF16, (8, P, NTT))

            v_lc = sync_load(Lc_f, bass.AP(agL_d, 0, [[64, 128], [8192, 8], [1, 64]]))
            wait("vector", "dl", v_lc)

            def emit_sq(src, dst):
                @on("vector")
                def _(e, src=src, dst=dst):
                    e.tensor_tensor(out=tA[:, :, 0:32], in0=src[:, :, 0:32],
                                    in1=src[:, :, 0:32], op=ALU.mult)
                    e.tensor_tensor(out=tA[:, :, 32:64], in0=src[:, :, 32:64],
                                    in1=src[:, :, 32:64], op=ALU.mult)
                    e.tensor_tensor(out=tB[:, :, 0:32], in0=src[:, :, 0:32],
                                    in1=src[:, :, 32:64], op=ALU.mult)
                    e.tensor_tensor(out=dst[:, :, 0:32], in0=tA[:, :, 0:32],
                                    in1=tA[:, :, 32:64], op=ALU.subtract)
                    return e.tensor_scalar(out=dst[:, :, 32:64], in0=tB[:, :, 0:32],
                                           scalar1=2.0, scalar2=None, op0=ALU.mult)
            emit_sq(Lc, Lg)
            emit_sq(Lg, GT)
            emit_sq(GT, Lg)
            emit_sq(Lg, GT)

            @on("vector")
            def _(e):
                return e.tensor_copy(out=Lg_f, in_=GT_f)

            def scan_step(src, dst_f, q):
                @on("vector")
                def _(e, src=src, dst_f=dst_f, q=q):
                    e.tensor_tensor(out=tA[:, :, 0:32], in0=src[:, :, 0:32],
                                    in1=Lc[:, :, 0:32], op=ALU.mult)
                    e.tensor_tensor(out=tA[:, :, 32:64], in0=src[:, :, 32:64],
                                    in1=Lc[:, :, 32:64], op=ALU.mult)
                    e.tensor_tensor(out=tB[:, :, 0:32], in0=tA[:, :, 0:32],
                                    in1=tA[:, :, 32:64], op=ALU.subtract)
                    e.tensor_tensor(out=tA[:, :, 0:32], in0=src[:, :, 0:32],
                                    in1=Lc[:, :, 32:64], op=ALU.mult)
                    e.tensor_tensor(out=tA[:, :, 32:64], in0=src[:, :, 32:64],
                                    in1=Lc[:, :, 0:32], op=ALU.mult)
                    e.tensor_tensor(out=tB[:, :, 32:64], in0=tA[:, :, 0:32],
                                    in1=tA[:, :, 32:64], op=ALU.add)
                    return e.tensor_tensor(out=dst_f, in0=tB_f, in1=FA[:, q, :],
                                           op=ALU.add)

            @on("vector")
            def _(e):
                return e.tensor_copy(out=sTf[0], in_=FA[:, 0, :])
            cur = 0
            for q in range(1, NTT):
                scan_step(sT[cur], sTf[1 - cur], q)
                cur = 1 - cur
            v_tloc = mark("vector")
            wait("gpsimd", "vs", v_tloc)
            gl_dma(bass.AP(s4st_d, 0, [[512, 128], [1, 512]]), sTf[cur])
            v_stg = sem["gl"].n
            wait("gpsimd", "gl", v_stg)

            @on("gpsimd")
            def _(e):
                return e.collective_compute(
                    "AllGather", ALU.bypass, replica_groups=[[0, 1, 2, 3], [4, 5, 6, 7]],
                    ins=[s4st_d.ap()], outs=[s4ag_d.ap()]).then_inc(sem["cc"].h, 1)
            v_ccSt = sem["cc"].inc()
            wait("gpsimd", "cc", v_ccSt)
            v_gath = gl_dma(gath, s4ag_d.ap().rearrange("s p c -> p s c"))

            wait("vector", "gl", v_gath)
            G = sT[1 - cur]
            G_f = sTf[1 - cur]

            @on("vector")
            def _(e, G_f=G_f):
                return e.memset(G_f, 0.0)
            for r in range(4):
                @on("vector")
                def _(e, r=r, G=G, G_f=G_f):
                    e.tensor_tensor(out=tA[:, :, 0:32], in0=G[:, :, 0:32],
                                    in1=Lg[:, :, 0:32], op=ALU.mult)
                    e.tensor_tensor(out=tA[:, :, 32:64], in0=G[:, :, 32:64],
                                    in1=Lg[:, :, 32:64], op=ALU.mult)
                    e.tensor_tensor(out=GT[:, :, 0:32], in0=tA[:, :, 0:32],
                                    in1=tA[:, :, 32:64], op=ALU.subtract)
                    e.tensor_tensor(out=tA[:, :, 0:32], in0=G[:, :, 0:32],
                                    in1=Lg[:, :, 32:64], op=ALU.mult)
                    e.tensor_tensor(out=tA[:, :, 32:64], in0=G[:, :, 32:64],
                                    in1=Lg[:, :, 0:32], op=ALU.mult)
                    e.tensor_tensor(out=GT[:, :, 32:64], in0=tA[:, :, 0:32],
                                    in1=tA[:, :, 32:64], op=ALU.add)
                    e.tensor_tensor(out=GT_f, in0=GT_f, in1=gath[:, r, :], op=ALU.add)
                    e.tensor_tensor(out=GT_f, in0=GT_f, in1=G_f, op=ALU.subtract)
                    return e.scalar_tensor_tensor(out=G_f, in0=GT_f,
                                                  scalar=pfx_sb[:, r:r + 1], in1=G_f,
                                                  op0=ALU.mult, op1=ALU.add)

            cur = 1 - cur
            v_tq = sem["ts"].n
            for q in range(NTT):
                wait("vector", "ts", v_tq)

                @on("vector")
                def _(e, cur=cur):
                    return e.tensor_copy(out=s16_f, in_=sTf[cur])
                v_rec = mark("vector")
                wait("tensor", "vs", v_rec)

                @on("tensor")
                def _(e):
                    mm = None
                    for g in range(8):
                        mm = e.transpose(pt16b[0:64, g * 128:(g + 1) * 128],
                                         s16[:, g, :], ident[:])
                    return mm.then_inc(sem["ts"].h, 1)
                v_tq = sem["ts"].inc()
                wait("vector", "ts", v_tq)

                @on("vector")
                def _(e, q=q):
                    return e.tensor_copy(
                        out=stk[0:64, :, :, q:q + 1],
                        in_=pt16b[0:64, :].rearrange("p (g c u) -> p g c u", g=8, u=1))
                mark("vector")
                if q < NTT - 1:
                    scan_step(sT[cur], sTf[1 - cur], q)
                    cur = 1 - cur
            v_scan_end = mark("vector")

            Tbuf = view(A3, 0, BF16, (2, 8, P))
            Ebuf = view(A3, 4096, BF16, (2, 8, P))
            ya = view(A4, 0, BF16, (16, 512))
            ya3 = view(A4, 0, BF16, (16, 32, 16))
            yb = view(A5, 0, BF16, (16, 512))
            yb3 = view(A5, 0, BF16, (16, 32, 16))
            NGC = 128
            wait("sync", "vs", v_scan_end)
            wait("gpsimd", "vs", v_scan_end)
            wait("scalar", "vs", v_scan_end)

            def t_src(g):
                return bass.AP(agT_d, g * 8 * 32768, [[255, 128], [32768, 8], [1, 128]])

            def e_src(g):
                return bass.AP(agE_d, g * 8 * 8192, [[128, 64], [8192, 8], [1, 128]])
            tl = {0: stream_ld("sync", TP, Tbuf[:, 0], t_src(0), None),
                  1: stream_ld("sync", TP, Tbuf[:, 1], t_src(1), None)}
            el = {0: stream_ld("gpsimd", GP, Ebuf[0:64, 0], e_src(0), None),
                  1: stream_ld("gpsimd", GP, Ebuf[0:64, 1], e_src(1), None)}
            cmm = {}
            for gi in range(NGC):
                if gi % 4 == 0:
                    wait_bank("tensor", (gi // 4) % 4)
                wait("tensor", tl[gi][0], tl[gi][1])
                wait("tensor", el[gi][0], el[gi][1])
                if gi < 2:
                    wait("tensor", "vs", v_scan_end)

                @on("tensor")
                def _(e, gi=gi):
                    mm = None
                    for i in range(8):
                        c = gi * 8 + i
                        b = (c // 32) % 4
                        col = (c % 32) * 16
                        mm = e.matmul(pb[b][:, col:col + 16], Tbuf[:, gi % 2, i, :],
                                      u16v[:, :, c:c + 1], start=(c % 32 == 0),
                                      stop=False, skip_group_check=True)
                        mm = e.matmul(pb[b][:, col:col + 16],
                                      Ebuf[0:64, gi % 2, i, :],
                                      stk[0:64, c // 128, c % 128, :],
                                      start=False, stop=(c % 32 == 31),
                                      skip_group_check=True)
                    return mm.then_inc(sem["ts"].h, 1)
                cmm[gi] = sem["ts"].inc()
                if gi % 4 == 3:
                    cb = (gi * 8) // 32
                    wait("scalar", "ts", cmm[gi])

                    @on("scalar")
                    def _(e, cb=cb):
                        tgt = ya if cb < 16 else yb
                        return e.activation(out=tgt[:, cb % 16, :],
                                            in_=pb[cb % 4][:], func=AF.Gelu)
                    bank_free[cb % 4] = ("ss", mark("scalar"))
                if gi + 2 < NGC:
                    tl[gi + 2] = stream_ld("sync", TP, Tbuf[:, gi % 2],
                                           t_src(gi + 2), cmm[gi])
                    el[gi + 2] = stream_ld("gpsimd", GP, Ebuf[0:64, gi % 2],
                                           e_src(gi + 2), cmm[gi])
            v_conv_end = sem["ts"].n
            v_gelu_end = sem["ss"].n

            ws4sb = view(A1, 0, BF16, (8, 2 * D))
            wait("sync", "ts", v_conv_end)
            wait("sync", "gl", sem["gl"].n)
            v_ws4 = sync_load(ws4sb, Ws4T_d.rearrange("(fc p) o -> p fc o", p=P))

            yT = view(A3, 0, BF16, (8, TOK))
            wait("tensor", "ss", v_gelu_end)
            prev_ev = sem["vs"].n
            for q in range(NTT):
                wait("tensor", "vs", prev_ev)

                @on("tensor")
                def _(e, q=q):
                    mm = None
                    for g in range(8):
                        src = ya3 if g < 4 else yb3
                        mm = e.transpose(pt16b[:, g * 128:(g + 1) * 128],
                                         src[:, (g % 4) * 4:(g % 4) * 4 + 4, :, q],
                                         ident[:])
                    return mm.then_inc(sem["ts"].h, 1)
                v_t = sem["ts"].inc()
                wait("vector", "ts", v_t)

                @on("vector")
                def _(e, q=q):
                    return e.tensor_copy(
                        out=yT[:, :, q * P:(q + 1) * P],
                        in_=pt16b[:].rearrange("p (g c) -> p g c", g=8))
                prev_ev = mark("vector")
            v_yT = prev_ev

            glus = view(A4, 0, F32, (D,))
            sgt = view(A4, 4096, F32, (D,))
            sgt2 = view(A4, 8192, F32, (D,))
            wait("tensor", "dl", v_ws4)
            wait("tensor", "vs", v_yT)
            for tb in range(NTT):
                for b in range(4):
                    wait_bank("tensor", b)

                @on("tensor")
                def _(e, tb=tb):
                    mm = None
                    for g in range(8):
                        for ob in range(4):
                            mm = e.matmul(pb[ob][:], yT[:, g, tb * P:(tb + 1) * P],
                                          ws4sb[:, g, ob * 512:(ob + 1) * 512],
                                          start=(g == 0), stop=(g == 7),
                                          skip_group_check=True)
                    return mm.then_inc(sem["ts"].h, 1)
                v_wm = sem["ts"].inc()
                wait("vector", "ts", v_wm)

                @on("vector")
                def _(e):
                    e.tensor_tensor(out=sgt[:, 0:512], in0=pb[2][:],
                                    in1=reps[:, 1, 0:512], op=ALU.add)
                    return e.tensor_tensor(out=sgt[:, 512:1024], in0=pb[3][:],
                                           in1=reps[:, 1, 512:1024], op=ALU.add)
                v_g1 = mark("vector")
                wait("scalar", "vs", v_g1)

                @on("scalar")
                def _(e):
                    return e.activation(out=sgt2, in_=sgt, func=AF.Sigmoid)
                v_g2 = mark("scalar")
                wait("vector", "ss", v_g2)

                @on("vector")
                def _(e, tb=tb):
                    e.tensor_tensor(out=glus[:, 0:512], in0=pb[0][:],
                                    in1=reps[:, 0, 0:512], op=ALU.add)
                    e.tensor_tensor(out=glus[:, 512:1024], in0=pb[1][:],
                                    in1=reps[:, 0, 512:1024], op=ALU.add)
                    e.tensor_tensor(out=glus, in0=glus, in1=sgt2, op=ALU.mult)
                    e.tensor_scalar(out=x_res[:, tb, :], in0=x_res[:, tb, :],
                                    scalar1=stat[:, 32 + tb:33 + tb],
                                    scalar2=stat[:, 48 + tb:49 + tb],
                                    op0=ALU.mult, op1=ALU.add)
                    return e.tensor_tensor(out=x_res[:, tb, :], in0=x_res[:, tb, :],
                                           in1=glus, op=ALU.add)
                v_fin = mark("vector")
                for b in range(4):
                    bank_free[b] = ("vs", v_fin)
            return None

        wait("vector", "dl", sem["dl"].n)
        wait("scalar", "dl", sem["dl"].n)

        v_ccS4 = emit_s4_prestage() if upto >= 5 else 0
        v_ccw = emit_moe_prestage() if upto >= 4 else 0
        if upto >= 1:
            emit_qffn()
        if upto >= 2:
            emit_attention()
        if upto >= 3:
            emit_qffn()
        if upto >= 4:
            emit_moe(v_ccw)
        if upto >= 5:
            emit_s4(v_ccS4)
        if upto >= 6:
            emit_moe(v_ccw)

        v_out_ready = mark("vector")
        wait("gpsimd", "vs", v_out_ready)

        @on("gpsimd")
        def _(e):
            return e.dma_start(out=out_t.rearrange("(tt p) d -> p tt d", p=P),
                               in_=x_res[:]).then_inc(sem["so"].h, 16)
        v_so = sem["so"].inc(16)

        @on("gpsimd")
        def _(e, v=v_so):
            e.wait_ge(sem["so"].h, v)

        block = es.enter_context(nc.Block())

        _NO_DRAIN = {"wait_ge", "wait_op", "nop", "drain", "sem_inc",
                     "dma_start", "dma_start_transpose", "print"}

        class _DrainProxy:

            def __init__(self, e):
                self._e = e

            def __getattr__(self, name):
                attr = getattr(self._e, name)
                if callable(attr) and name not in _NO_DRAIN and not name.startswith("_"):
                    def wrapped(*a, _attr=attr, **k):
                        self._e.drain()
                        return _attr(*a, **k)
                    return wrapped
                return attr

        def _run(e, eng):
            if eng in ("vector", "scalar"):
                e = _DrainProxy(e)
            for f in ops[eng]:
                r = f(e)
                if r is not None:
                    last_inst[eng] = r

        @block.sync
        def _(e):
            _run(e, "sync")

        @block.scalar
        def _(e):
            _run(e, "scalar")

        @block.vector
        def _(e):
            _run(e, "vector")

        @block.tensor
        def _(e):
            _run(e, "tensor")

        @block.gpsimd
        def _(e):
            _run(e, "gpsimd")

    return nc


def make_in_maps(inputs):
    import ml_dtypes
    x = np.asarray(inputs["L0_attn_x"])
    ident16 = np.eye(P, dtype=ml_dtypes.bfloat16)
    shared = {
        "Wq": np.asarray(inputs["L0_attn_Wq"]), "Wk": np.asarray(inputs["L0_attn_Wk"]),
        "Wv": np.asarray(inputs["L0_attn_Wv"]), "Wo": np.asarray(inputs["L0_attn_Wo"]),
        "bo": np.asarray(inputs["L0_attn_bo"]),
        "W8": np.asarray(inputs["L0_ffn_weight_q"]),
        "wscale": np.asarray(inputs["L0_ffn_weight_scale"]),
        "qbias": np.asarray(inputs["L0_ffn_bias"]),
        "Ws4": np.asarray(inputs["L1_attn_W"]), "bs4": np.asarray(inputs["L1_attn_b"]),
        "rw": np.asarray(inputs["L1_ffn_router_w"]), "rb": np.asarray(inputs["L1_ffn_router_b"]),
        "b1": np.asarray(inputs["L1_ffn_b1"]),
        "b2": np.asarray(inputs["L1_ffn_b2"]),
        "lnw": np.asarray(inputs["L1_ffn_ln_w"]), "lnb": np.asarray(inputs["L1_ffn_ln_b"]),
        "ident16": ident16,
        "masku4": np.tile((np.arange(P)[:, None] <= np.arange(P)[None, :]), (1, 4)).astype(ml_dtypes.bfloat16),
        "onesk": np.ones((P, HD), dtype=ml_dtypes.bfloat16),
        "ones1": np.ones((P, 1), dtype=ml_dtypes.bfloat16),
    }
    maps = []
    for core in range(8):
        b, q = core // 4, core % 4
        pfx = np.zeros((P, 4), np.float32)
        pfx[:, :q] = 1.0
        m = dict(shared)
        m["x"] = np.ascontiguousarray(x[b, q * TOK:(q + 1) * TOK, :])
        m["pfx"] = pfx
        m["myw1"] = np.ascontiguousarray(np.asarray(inputs["L1_ffn_w1"])[core])
        m["myw2"] = np.ascontiguousarray(np.asarray(inputs["L1_ffn_w2"])[core])
        cs = slice(core * P, (core + 1) * P)
        m["my_logdt"] = np.ascontiguousarray(
            np.asarray(inputs["L1_attn_log_dt"])[cs].reshape(P, 1))
        m["my_Cre"] = np.ascontiguousarray(np.asarray(inputs["L1_attn_C_re"])[cs])
        m["my_Cim"] = np.ascontiguousarray(np.asarray(inputs["L1_attn_C_im"])[cs])
        m["my_logAre"] = np.ascontiguousarray(np.asarray(inputs["L1_attn_log_A_re"])[cs])
        m["my_Aim"] = np.ascontiguousarray(np.asarray(inputs["L1_attn_A_im"])[cs])
        m["my_D"] = np.ascontiguousarray(
            np.asarray(inputs["L1_attn_D"])[cs].reshape(P, 1))
        maps.append(m)
    return maps


def kernel(**inputs):
    nc = build_nc()
    in_maps = make_in_maps(inputs)
    res = run_bass_kernel_spmd(nc, in_maps, core_ids=list(range(8)))
    outs = [res.results[i]["out"] for i in range(8)]
    full = np.stack([np.concatenate(outs[0:4], axis=0),
                     np.concatenate(outs[4:8], axis=0)])
    return full.astype(np.float32)
